# version 50
# baseline (speedup 1.0000x reference)
import contextlib
import numpy as np

import concourse.bass as bass
import concourse.mybir as mybir
from concourse.bass_utils import run_bass_kernel_spmd

F32 = mybir.dt.float32
BF16 = mybir.dt.bfloat16
I32 = mybir.dt.int32
AF = mybir.ActivationFunctionType
ALU = mybir.AluOpType
AX = mybir.AxisListType

MAGIC = 3.0 * 2.0 ** 22
C0 = 16.0
EPS = 1e-5
BASE = 16512


class Emitter:
    ENGINES = ("sync", "scalar", "vector", "tensor", "gpsimd")

    def __init__(self, nc, ctx):
        self.nc = nc
        self.ops = {e: [] for e in self.ENGINES}
        self.sems = {}
        self.cnt = {}
        self.last_wait = {e: {} for e in self.ENGINES}
        self.future_waits = []
        self.ctx = ctx

    def sem(self, name):
        if name not in self.sems:
            self.sems[name] = self.ctx.enter_context(self.nc.semaphore(name))
            self.cnt[name] = 0
        return self.sems[name]

    def c(self, name):
        self.sem(name)
        return self.cnt[name]

    def emit(self, eng, meth, *args, inc=None, **kw):
        if inc is not None:
            s, n = inc
            self.sem(s)
            self.cnt[s] += n
            self.ops[eng].append((meth, args, kw, (self.sems[s], n)))
            return self.cnt[s]
        self.ops[eng].append((meth, args, kw, None))
        return None

    def wait(self, eng, semname, val, future=False):
        if val <= 0:
            return
        self.sem(semname)
        if future:
            self.future_waits.append((eng, semname, val))
        else:
            assert val <= self.cnt[semname], \
                f"wait {eng} {semname}>={val} but only {self.cnt[semname]} emitted"
        lw = self.last_wait[eng].get(semname, -1)
        if val <= lw:
            return
        self.last_wait[eng][semname] = val
        self.ops[eng].append(("wait_ge", (self.sems[semname], val), {}, None))

    def check_future(self):
        for eng, s, v in self.future_waits:
            assert v <= self.cnt[s], \
                f"future wait {eng} {s}>={v} never reached (final {self.cnt[s]})"

    def dma(self, eng, out, in_, semname, **kw):
        return self.emit(eng, "dma_start", out, in_, inc=(semname, 16), **kw)

    def replay(self):
        with self.nc.Block() as block:
            def mk(name):
                def body(eng):
                    for meth, args, kw, inc in self.ops[name]:
                        if meth == "wait_ge":
                            eng.wait_ge(*args)
                        else:
                            ins = getattr(eng, meth)(*args, **kw)
                            if inc is not None:
                                ins.then_inc(*inc)
                return body
            block.sync(mk("sync"))
            block.scalar(mk("scalar"))
            block.vector(mk("vector"))
            block.tensor(mk("tensor"))
            block.gpsimd(mk("gpsimd"))


def build(T=16, NST=2, debug=False, phase=3, attn_lvl=4):
    assert T % 4 == 0
    N = T * 128
    QC = N // 512
    ST = T // NST
    assert ST % 4 == 0
    NTG = ST // 4

    nc = bass.Bass("TRN2", target_bir_lowering=False)

    x_d = nc.dram_tensor("x", [N, 1024], F32, kind="ExternalInput")
    wqkv_d = nc.dram_tensor("wqkv", [8, 128, 384], F32, kind="ExternalInput")
    wproj_d = nc.dram_tensor("wproj", [8, 128, 128], F32, kind="ExternalInput")
    bproj_d = nc.dram_tensor("bproj", [1, 1024], F32, kind="ExternalInput")
    wg_d = nc.dram_tensor("wg", [4096, 1024], F32, kind="ExternalInput")
    wu_d = nc.dram_tensor("wu", [4096, 1024], F32, kind="ExternalInput")
    wd_d = nc.dram_tensor("wd", [1024, 4096], F32, kind="ExternalInput")
    cst_d = nc.dram_tensor("cst", [128, 8], F32, kind="ExternalInput")
    y_d = nc.dram_tensor("y", [N, 1024], F32, kind="ExternalOutput")

    sk = "ExternalOutput" if debug else "Internal"
    xbf_s = nc.dram_tensor("xbf_s", [N, 1024], BF16, kind=sk)
    xt_s = nc.dram_tensor("xt_s", [N, 1024], F32, kind=sk)
    xp_s = nc.dram_tensor("xp_s", [N, 1024], F32, kind=sk)
    xq_s = nc.dram_tensor("xq_s", [N, 1024], BF16, kind=sk)
    wgq_s = nc.dram_tensor("wgq_s", [4096, 1024], BF16, kind=sk)
    wuq_s = nc.dram_tensor("wuq_s", [4096, 1024], BF16, kind=sk)
    wdq_s = nc.dram_tensor("wdq_s", [1024, 4096], BF16, kind=sk)
    wgqT_s = nc.dram_tensor("wgqT_s", [1024, 4096], BF16, kind="Internal")
    wuqT_s = nc.dram_tensor("wuqT_s", [1024, 4096], BF16, kind="Internal")
    wdqT_s = nc.dram_tensor("wdqT_s", [4096, 1024], BF16, kind="Internal")
    hq_s = nc.dram_tensor("hq_s", [N, 4096], BF16, kind=sk)
    if debug:
        dbg_qp = nc.dram_tensor("dbg_qp", [128, 8 * 2048], BF16, kind="ExternalOutput")
        dbg_kp = nc.dram_tensor("dbg_kp", [128, 8 * 2048], BF16, kind="ExternalOutput")
        dbg_va = nc.dram_tensor("dbg_va", [128, T * 1040], BF16, kind="ExternalOutput")
        dbg_ar = nc.dram_tensor("dbg_ar", [128, 4096], BF16, kind="ExternalOutput")
        dbg_rc = nc.dram_tensor("dbg_rc", [128, 128], F32, kind="ExternalOutput")
        dbg_wsc = nc.dram_tensor("dbg_wsc", [128, 8], F32, kind="ExternalOutput")

    off = [BASE]

    def alloc(name, shape, dtype, at=None):
        nbytes = int(np.prod(shape[1:])) * mybir.dt.size(dtype)
        nbytes = (nbytes + 31) // 32 * 32
        if at is None:
            at = off[0]
            off[0] = at + nbytes
        assert at + nbytes <= 229344, f"SBUF overflow: {name}"
        return nc.alloc_sbuf_tensor_at(name, list(shape), dtype, offset=at), at

    wqkv_sb, _ = alloc("wqkv_sb", [128, 3072], BF16)
    wproj_sb, _ = alloc("wproj_sb", [128, 1024], BF16)
    bpb, bpbo = alloc("bpb", [128, 1024], F32)
    wtstage, _ = alloc("wtstage", [128, 2048], BF16, at=bpbo)
    cst, _ = alloc("cst_sb", [128, 8], F32)
    onescol, _ = alloc("onescol", [128, 1], F32)
    onesrow = None
    a_sb, _ = alloc("a_sb", [128, 16], F32)
    ln_sb, _ = alloc("ln_sb", [128, 16], F32)
    wsc, _ = alloc("wsc", [128, 8], F32)
    scg, _ = alloc("scg", [128, 16], F32)
    scu, _ = alloc("scu", [128, 16], F32)
    scd, _ = alloc("scd", [128, 16], F32)
    wt1, _ = alloc("wt1", [128, 8], F32)

    A0, A0o = alloc("A0", [128, 2048], F32)
    A1, A1o = alloc("A1", [128, 8 * 2048], BF16)
    A2, A2o = alloc("A2", [128, 8 * 2048], BF16)
    A3, A3o = alloc("A3", [128, 8 * 2048], BF16)
    A4, A4o = alloc("A4", [128, T * 1040], BF16)
    A5, A5o = alloc("A5", [128, 2048], F32)
    A6, A6o = alloc("A6", [128, 2048], BF16)
    A7, A7o = alloc("A7", [128, 2048], BF16)
    A8, A8o = alloc("A8", [128, 4096], BF16)
    A9, A9o = alloc("A9", [128, 1024], F32)
    A10, _ = alloc("A10", [128, 128], F32)
    A11, A11o = alloc("A11", [128, 512], F32)
    A12, _ = alloc("A12", [128, 1024], F32)
    A13, A13o = alloc("A13", [128, 2048], F32)
    A14, A14o = alloc("A14", [128, 2048], BF16)
    wdt2_sb, _ = alloc("wdt2_sb", [128, 2048], BF16, at=A14o)
    A15, A15o = alloc("A15", [128, 3072], F32)
    onesrow, _ = alloc("onesrow", [1, 128], F32, at=A11o)

    wg_sl = [alloc("wg_sl0", [128, 4096], BF16, at=A0o)[0],
             alloc("wg_sl1", [128, 4096], BF16, at=A8o)[0]]
    wu_sl0, _ = alloc("wu_sl0", [128, 4096], BF16, at=A5o)
    wu_sl1a, _ = alloc("wu_sl1a", [128, 2048], BF16, at=A6o)
    wu_sl1b, _ = alloc("wu_sl1b", [128, 2048], BF16, at=A7o)
    wdt_sb, _ = alloc("wdt_sb", [128, 2048], BF16, at=A9o)
    xre_f, _ = alloc("xre_f", [128, 2048], F32, at=A13o)
    h_t = []
    for t in range(8):
        base_off = A3o if t < 4 else A4o
        hh, _ = alloc(f"h_{t}", [128, 4096], BF16, at=base_off + (t % 4) * 8192)
        h_t.append(hh)
    hqT, _ = alloc("hqT", [128, 16384], BF16, at=A2o)
    xqT, _ = alloc("xqT", [128, 16384], BF16, at=A1o)

    assert off[0] <= 229344, f"SBUF static overflow {off[0]}"

    ctx = contextlib.ExitStack()
    ps = ctx.enter_context(nc.psum_tensor("ps", [128, 4096], F32))

    def bank(i, lo=0, hi=512):
        return ps[:, 512 * i + lo: 512 * i + hi]

    PROJ_B = [A15[:, 0:1024], A15[:, 1024:2048], A15[:, 2048:3072],
              A12[:, 0:1024]]

    E = Emitter(nc, ctx)
    em, wait, dma, c = E.emit, E.wait, E.dma, E.c

    def finish():
        for name in sorted(E.cnt):
            if name.startswith("s_") and E.cnt[name] > 0:
                wait("sync", name, E.cnt[name])
        E.replay()
        ctx.close()
        return nc


    dma("scalar", cst[:, :], cst_d[:, :], "s_cst")
    n = dma("scalar", A12[0:1, 0:1024], bproj_d[:, :], "s_bp")

    def emit_w2(j):
        for g in (2 * j, 2 * j + 1):
            if g < 8:
                par = g % 2
                st_ap = A13[:, par * 1024: par * 1024 + 384]
                if g >= 2:
                    wait("scalar", "c_wqc", g - 1)
                nn = dma("scalar", st_ap, wqkv_d[g], f"s_wqv{par}")
                wait("scalar", f"s_wqv{par}", nn)
                em("scalar", "copy", wqkv_sb[:, g * 384:(g + 1) * 384], st_ap,
                   inc=("c_wqc", 1))
            else:
                gp = g - 8
                par = gp % 2
                st_ap = A13[:, par * 1024: par * 1024 + 128]
                if gp < 2:
                    wait("scalar", "c_wqc", 8)
                else:
                    wait("scalar", "c_wpc", gp - 1)
                nn = dma("scalar", st_ap, wproj_d[gp], f"s_wpv{par}")
                wait("scalar", f"s_wpv{par}", nn)
                em("scalar", "copy", wproj_sb[:, gp * 128:(gp + 1) * 128], st_ap,
                   inc=("c_wpc", 1))

    em("vector", "memset", onescol[:, :], 1.0, inc=("c_ones", 1))
    em("vector", "memset", onesrow[:, :], 1.0, inc=("c_ones", 1))
    for t in range(T):
        va = A4[:, t * 1040:(t + 1) * 1040]
        em("vector", "memset",
           va.rearrange("p (h d) -> p h d", h=16)[:, :, 64:65], 1.0, inc=("c_vone", 1))

    wait("tensor", "s_bp", 16)
    wait("tensor", "c_ones", 2)
    em("tensor", "matmul", bank(6), onesrow[0:1, :], A12[0:1, 0:512],
       start=True, stop=True)
    em("tensor", "matmul", bank(7), onesrow[0:1, :], A12[0:1, 512:1024],
       start=True, stop=True, inc=("m_bpb", 1))
    wait("scalar", "m_bpb", 1)
    em("scalar", "copy", bpb[:, 0:512], bank(6))
    em("scalar", "copy", bpb[:, 512:1024], bank(7), inc=("c_bpb", 1))

    def xslot(i):
        p4 = i % 4
        if p4 < 2:
            return A0[:, p4 * 1024:(p4 + 1) * 1024]
        return A8.bitcast(F32)[:, (p4 - 2) * 1024:(p4 - 1) * 1024]

    NXC = max(1, N // 512)
    _w2_next = [0]
    _chunk_at = {}
    for k2 in range(NXC):
        _chunk_at.setdefault(min(T - 1, 4 * k2 + 4), []).append(k2)
    for i in range(T):
        p4 = i % 4
        par = i % 2
        xst = xslot(i)
        xbst = A14[:, par * 1024:(par + 1) * 1024]
        if i >= 4:
            wait("sync", f"s_xts{p4}", 16 * (i // 4))
        nn = dma("sync", xst, x_d[i * 128:(i + 1) * 128, :], f"s_x{p4}")
        wait("scalar", f"s_x{p4}", nn)
        wait("scalar", "c_bpb", 1)
        if i >= 2:
            wait("scalar", f"s_xbf{par}", 16 * (i // 2))
        em("scalar", "copy", xbst, xst, inc=("c_xc", 1))
        wait("vector", f"s_x{p4}", nn)
        wait("vector", "c_bpb", 1)
        wait("vector", "c_xc", i + 1)
        em("vector", "tensor_add", xst, xst, bpb[:, :], inc=("c_xa", 1))
        wait("scalar", "c_xc", i + 1)
        dma("scalar", xbf_s[i * 128:(i + 1) * 128, :], xbst, f"s_xbf{par}")
        wait("gpsimd", "c_xa", i + 1)
        dma("gpsimd", xt_s[i * 128:(i + 1) * 128, :], xst, f"s_xts{p4}")
        if i >= 1 and _w2_next[0] < 8:
            emit_w2(_w2_next[0])
            _w2_next[0] += 1
        if i == T - 1:
            while _w2_next[0] < 8:
                emit_w2(_w2_next[0])
                _w2_next[0] += 1
        for k2 in _chunk_at.get(i, []):
            need = min(T, 4 * (k2 + 1))
            wait("sync", "s_xbf0", 16 * ((need + 1) // 2))
            wait("sync", "s_xbf1", 16 * (need // 2))
            r0, r1 = k2 * 512, min(N, k2 * 512 + 512)
            for g in range(8):
                if k2:
                    wait("sync", f"s_xt{g}", 16 * k2)
                dma("sync", A1[:, g * 2048 + r0: g * 2048 + r1],
                    xbf_s[r0:r1, g * 128:(g + 1) * 128], f"s_xt{g}",
                    transpose=True)

    def w_tiles(w):
        R, Cc = w.shape
        return [w[r * 128:(r + 1) * 128, cc * 1024:(cc + 1) * 1024]
                for r in range(R // 128) for cc in range(Cc // 1024)]

    wmats = [wg_d, wu_d, wd_d]
    acc_slot = [wt1[:, 0:1], wt1[:, 1:2], wt1[:, 2:3]]
    part_slot = wt1[:, 3:4]

    def emit_wmean():
        def stg_ap(par):
            if par < 2:
                return A5[:, par * 1024:(par + 1) * 1024]
            if par < 4:
                return A13[:, (par - 2) * 1024:(par - 1) * 1024]
            return PROJ_B[par - 4]
        for mi, w_d2 in enumerate(wmats):
            for ci, tl in enumerate(w_tiles(w_d2)):
                gidx = mi * 32 + ci
                par = gidx % 8
                stg = stg_ap(par)
                if gidx < 8:
                    wait("sync", "c_wpc", 8)
                if gidx >= 8 and gidx % 4 == 0:
                    wait("sync", "c_wma", gidx - 4, future=True)
                sm = ("s_wm0", "s_wm1", "s_wm2", "s_wm3", "s_wqv0",
                      "s_wqv1", "s_wpv0", "s_wpv1")[par]
                nn = dma("sync", stg, tl, sm)
                wait("vector", sm, nn)
                wait("vector", "c_wma", c("c_wma"))
                em("vector", "reduce_sum", part_slot, stg, AX.X,
                   apply_absolute_value=True, inc=("c_wmr", 1))
                wait("vector", "c_wmr", c("c_wmr"))
                if ci == 0:
                    em("vector", "tensor_copy", acc_slot[mi], part_slot,
                       inc=("c_wma", 1))
                else:
                    em("vector", "tensor_add", acc_slot[mi], acc_slot[mi],
                       part_slot, inc=("c_wma", 1))

    def emit_wsum():
        wait("tensor", "c_bpb", 1)
        wait("tensor", "c_exp", c("c_exp"))
        wait("tensor", "c_wma", c("c_wma"))
        for mi in range(3):
            em("tensor", "matmul", ps[0:1, 3136 + mi:3137 + mi], onescol[:, 0:1],
               acc_slot[mi], start=True, stop=True, inc=("m_wsum", 1))
        wait("scalar", "m_wsum", 3)
        em("scalar", "copy", wt1[0:1, 4:7], ps[0:1, 3136:3139], inc=("c_wse", 1))
        wait("vector", "c_wse", 1)
        em("vector", "tensor_scalar", wt1[0:1, 4:7], wt1[0:1, 4:7],
           1.0 / (4096.0 * 1024.0), 1e-5, ALU.mult, ALU.max, inc=("c_wv", 1))
        wait("vector", "c_wv", 1)
        em("vector", "reciprocal", wt1[0:1, 0:3], wt1[0:1, 4:7], inc=("c_wv", 1))
        wait("tensor", "c_wv", 2)
        em("tensor", "matmul", ps[:, 3139:3142], onesrow[0:1, :], wt1[0:1, 0:3],
           start=True, stop=True)
        em("tensor", "matmul", ps[:, 3142:3145], onesrow[0:1, :], wt1[0:1, 4:7],
           start=True, stop=True, inc=("m_wbc", 1))
        wait("scalar", "m_wbc", 1)
        em("scalar", "copy", wsc[:, 0:3], ps[:, 3139:3142])
        em("scalar", "copy", wsc[:, 3:6], ps[:, 3142:3145], inc=("c_wsc", 1))

    wq_scr = [wgq_s, wuq_s, wdq_s]
    _wq_units = []
    for mi in (2, 0, 1):
        for tii, to in zip(w_tiles(wmats[mi]), w_tiles(wq_scr[mi])):
            _wq_units.append((mi, tii, to))
    _wq_emitted = [0]

    def emit_wq_load(gidx):
        mi, tii, to = _wq_units[gidx]
        par = gidx % 2
        stg = A5[:, par * 1024:(par + 1) * 1024]
        if gidx >= 2:
            wait("sync", "c_wq3", 3 * (gidx - 1), future=True)
        else:
            wait("sync", "c_wma", 96, future=True)
        dma("sync", stg, tii, f"s_wqp{par}")

    def drain_wq(k):
        for _ in range(k):
            gidx = _wq_emitted[0]
            if gidx >= len(_wq_units):
                return
            _wq_emitted[0] = gidx + 1
            mi, tii, to = _wq_units[gidx]
            par = gidx % 2
            stg = A5[:, par * 1024:(par + 1) * 1024]
            obuf = A6[:, par * 1024:(par + 1) * 1024]
            if gidx == 0:
                emit_wq_load(0)
                emit_wq_load(1)
            if gidx + 2 < len(_wq_units):
                emit_wq_load(gidx + 2)
            wait("vector", f"s_wqp{par}", 16 * (gidx // 2 + 1))
            wait("vector", "c_wsc", 1)
            wait("vector", "c_wq3", c("c_wq3"))
            em("vector", "tensor_scalar", stg, stg, wsc[:, mi:mi + 1], 1.0,
               ALU.mult, ALU.min, inc=("c_wq3", 1))
            wait("vector", "c_wq3", c("c_wq3"))
            em("vector", "tensor_scalar", stg, stg, -1.0, MAGIC, ALU.max, ALU.add,
               inc=("c_wq3", 1))
            wait("vector", "c_wq3", c("c_wq3"))
            if gidx >= 2:
                wait("vector", f"s_wqs{par}", 16 * (gidx // 2))
            em("vector", "tensor_scalar", obuf, stg, MAGIC, None, ALU.subtract,
               inc=("c_wq3", 1))
            wait("sync", "c_wq3", c("c_wq3"), future=True)
            dma("sync", to, obuf, f"s_wqs{par}")

    emit_wmean()

    if phase == 0:
        return finish()

    _wsum_done = [False]

    _wt_units = list(range(32))
    _wt_issue = [0]
    _wt_pend = []

    def _wt_store_flush():
        if _wt_pend:
            idx2, ic3 = _wt_pend.pop(0)
            par2 = idx2 % 2
            buf2 = wtstage[:, par2 * 1024:(par2 + 1) * 1024]
            wait("gpsimd", f"s_wtt{par2}", 16 * (idx2 // 2 + 1))
            dma("gpsimd", wdqT_s[ic3 * 128:(ic3 + 1) * 128, :], buf2,
                f"s_wtb{par2}")

    def drain_wt(k):
        for _ in range(k):
            if not _wt_units:
                return
            if _wq_emitted[0] < 32:
                return
            ic2 = _wt_units.pop(0)
            idx = _wt_issue[0]
            _wt_issue[0] = idx + 1
            par = idx % 2
            buf = wtstage[:, par * 1024:(par + 1) * 1024]
            wait("sync", "c_xa", T)
            wait("sync", "s_wqs0", 16 * 16, future=True)
            wait("sync", "s_wqs1", 16 * 16, future=True)
            if idx >= 2:
                wait("sync", f"s_wtb{par}", 16 * (idx // 2))
            dma("sync", buf, wdq_s[0:1024, ic2 * 128:(ic2 + 1) * 128],
                f"s_wtt{par}", transpose=True)
            _wt_pend.append((idx, ic2))
            if len(_wt_pend) > 1:
                _wt_store_flush()

    _xqt_next = [0]

    def emit_xqt_chunk(qc2):
        _xqt_next[0] = qc2 + 1
        r0, r2 = qc2 * 512, min(N, qc2 * 512 + 512)
        ntile = r2 // 128
        wait("sync", "e_qkv", 16 * QC)
        wait("sync", "e_v", 3 * T)
        wait("sync", "s_xqs0", 16 * ((ntile + 1) // 2))
        wait("sync", "s_xqs1", 16 * (ntile // 2))
        if qc2:
            wait("sync", "s_xqt", 16 * 8 * qc2)
        for g in range(8):
            dma("sync", xqT[:, g * 2048 + r0: g * 2048 + r2],
                xq_s[r0:r2, g * 128:(g + 1) * 128], "s_xqt", transpose=True)

    wait("tensor", "c_wqc", 8)
    for qt in range(QC):
        for cc in range(16):
            ch0 = cc * 128 if cc < 8 else 1024 + (cc - 8) * 128
            g, o = ch0 // 384, ch0 % 384
            dest = A2 if cc < 8 else A3
            p = cc % 8
            idx = qt * 16 + cc
            b = idx % 2
            if idx >= 2:
                wait("tensor", "e_qkv", idx - 1)
            wait("tensor", f"s_xt{g}", 16 * (qt + 1))
            em("tensor", "matmul", bank(b), wqkv_sb[:, g * 384 + o: g * 384 + o + 128],
               A1[:, g * 2048 + qt * 512: g * 2048 + qt * 512 + 512],
               start=True, stop=True, inc=("m_qkv", 1))
            wait("scalar", "m_qkv", c("m_qkv"))
            wait("scalar", "s_cst", 16)
            dap = dest[:, p * 2048 + qt * 512: p * 2048 + qt * 512 + 512]
            if cc < 8:
                em("scalar", "activation", dap, bank(b), AF.Copy, scale=cst[:, 1:2],
                   inc=("e_qkv", 1))
            else:
                em("scalar", "copy", dap, bank(b), inc=("e_qkv", 1))
    wait("tensor", "e_qkv", 16 * QC)
    vgroups = [(5, 128, 256), (6, 0, 384), (7, 0, 384)]
    for t in range(T):
        for gi, (g, o0, nv) in enumerate(vgroups):
            idx = t * 3 + gi
            b = idx % 2
            if idx >= 2:
                wait("tensor", "e_v", idx - 1)
            wait("tensor", f"s_xt{g}", 16 * (t // 4 + 1))
            em("tensor", "matmul", bank(b, 0, nv),
               A1[:, g * 2048 + t * 128: g * 2048 + t * 128 + 128],
               wqkv_sb[:, g * 384 + o0: g * 384 + o0 + nv],
               start=True, stop=True, inc=("m_v", 1))
            vch0 = g * 384 + o0 - 2048
            h0, nh = vch0 // 64, nv // 64
            va = A4[:, t * 1040:(t + 1) * 1040]
            wait("scalar", "m_v", c("m_v"))
            em("scalar", "copy",
               va[:, 65 * h0: 65 * h0 + 65 * nh].rearrange("p (h d) -> p h d", h=nh)[:, :, 0:64],
               bank(b, 0, nv).rearrange("p (h d) -> p h d", h=nh),
               inc=("e_v", 1))

    if phase == 1:
        return finish()

    wait("tensor", "e_v", 3 * T)
    wait("tensor", "c_vone", T)

    def emit_xre_reload(ti):
        par = ti % 2
        if ti < 2:
            wait("scalar", "c_xc", T)
            wait("scalar", "c_xa", T)
            for p4_ in range(4):
                wait("scalar", f"s_xts{p4_}", 16 * ((T + 3 - p4_) // 4))
            wait("scalar", "s_xbf0", 16 * (T // 2))
            wait("scalar", "s_xbf1", 16 * (T // 2))
        else:
            wait("scalar", "c_radd", c("c_radd"))
        E.dma("scalar", A0[:, par * 1024:(par + 1) * 1024],
              xt_s[ti * 128:(ti + 1) * 128, :], f"s_xr{par}")


    def emit_ln_quad(qc):
        t0 = qc * 4
        sq = A13[:, 0:1024]
        wait("vector", "c_radd", c("c_radd"))
        wait("vector", "c_v", c("c_v"))
        for j in range(4):
            em("vector", "reduce_sum", ln_sb[:, j:j + 1], PROJ_B[j], AX.X,
               inc=("c_v", 1))
        wait("vector", "c_v", c("c_v"))
        em("vector", "tensor_scalar", ln_sb[:, 8:12], ln_sb[:, 0:4],
           -1.0 / 1024.0, None, ALU.mult, inc=("c_v", 1))
        for j in range(4):
            wait("vector", "c_v", c("c_v"))
            em("vector", "tensor_scalar", PROJ_B[j], PROJ_B[j],
               ln_sb[:, 8 + j:9 + j], None, ALU.add, inc=("c_v", 1))
            wait("vector", "c_v", c("c_v"))
            em("vector", "tensor_mul", sq, PROJ_B[j], PROJ_B[j],
               inc=("c_v", 1))
            wait("vector", "c_v", c("c_v"))
            em("vector", "reduce_sum", ln_sb[:, 4 + j:5 + j], sq, AX.X,
               inc=("c_v", 1))
        wait("vector", "c_v", c("c_v"))
        em("vector", "tensor_scalar", ln_sb[:, 4:8], ln_sb[:, 4:8],
           1.0 / 1024.0, EPS, ALU.mult, ALU.add, inc=("c_v", 1))
        wait("vector", "c_v", c("c_v"))
        em("vector", "tensor_scalar", scd.bitcast(I32)[:, 0:4],
           ln_sb.bitcast(I32)[:, 4:8], 1, None, ALU.arith_shift_right,
           inc=("c_v", 1))
        wait("vector", "c_v", c("c_v"))
        em("vector", "tensor_scalar", ln_sb.bitcast(I32)[:, 12:16],
           scd.bitcast(I32)[:, 0:4], -1, 0x5F3759DF, ALU.mult, ALU.add,
           inc=("c_v", 1))
        for _ in range(4):
            wait("vector", "c_v", c("c_v"))
            em("vector", "tensor_mul", scd[:, 0:4], ln_sb[:, 12:16],
               ln_sb[:, 12:16], inc=("c_v", 1))
            wait("vector", "c_v", c("c_v"))
            em("vector", "tensor_mul", scd[:, 4:8], scd[:, 0:4],
               ln_sb[:, 4:8], inc=("c_v", 1))
            wait("vector", "c_v", c("c_v"))
            em("vector", "tensor_scalar", scd[:, 0:4], scd[:, 4:8], -0.5, 1.5,
               ALU.mult, ALU.add, inc=("c_v", 1))
            wait("vector", "c_v", c("c_v"))
            em("vector", "tensor_mul", ln_sb[:, 12:16], ln_sb[:, 12:16],
               scd[:, 0:4], inc=("c_v", 1))
        for jp in range(2):
            for j in (2 * jp, 2 * jp + 1):
                ti = t0 + j
                par = ti % 2
                wait("vector", "c_v", c("c_v"))
                em("vector", "tensor_scalar", PROJ_B[j], PROJ_B[j],
                   ln_sb[:, 12 + j:13 + j], None, ALU.mult, inc=("c_v", 1))
                wait("gpsimd", "c_v", c("c_v"))
                if ti >= 2:
                    wait("gpsimd", f"s_xps{par}", 16 * (ti // 2))
                E.dma("gpsimd", xp_s[ti * 128:(ti + 1) * 128, :], PROJ_B[j],
                      f"s_xps{par}")
                wait("vector", "c_v", c("c_v"))
                em("vector", "reduce_max", scg[:, 8 + j:9 + j], PROJ_B[j],
                   AX.X, apply_absolute_value=True, inc=("c_v", 1))
            c0 = 8 + 2 * jp
            wait("vector", "c_v", c("c_v"))
            em("vector", "tensor_scalar", scg[:, 2 * jp:2 * jp + 2],
               scg[:, c0:c0 + 2], 1e-5, None, ALU.max, inc=("c_v", 1))
            wait("vector", "c_v", c("c_v"))
            em("vector", "reciprocal", scg[:, 4 + 2 * jp:6 + 2 * jp],
               scg[:, 2 * jp:2 * jp + 2], inc=("c_v", 1))
            wait("vector", "c_v", c("c_v"))
            em("vector", "tensor_scalar", scu[:, 2 * jp:2 * jp + 2],
               scg[:, 4 + 2 * jp:6 + 2 * jp], 127.0, None, ALU.mult,
               inc=("c_v", 1))
            em("vector", "tensor_scalar", a_sb[:, t0 + 2 * jp:t0 + 2 * jp + 2],
               scg[:, 2 * jp:2 * jp + 2], 1.0 / 127.0, None, ALU.mult,
               inc=("c_v", 1))
            for j in (2 * jp, 2 * jp + 1):
                ti = t0 + j
                par = ti % 2
                xph = A13[:, par * 1024:(par + 1) * 1024]
                xqst = A14[:, par * 1024:(par + 1) * 1024]
                wait("vector", "c_v", c("c_v"))
                em("vector", "tensor_scalar", xph, PROJ_B[j],
                   scu[:, j:j + 1], MAGIC, ALU.mult, ALU.add, inc=("c_v", 1))
                wait("vector", "c_v", c("c_v"))
                if ti >= 2:
                    wait("vector", f"s_xqs{par}", 16 * (ti // 2))
                em("vector", "tensor_scalar", xqst, xph, MAGIC, None,
                   ALU.subtract, inc=("c_v", 1))
                wait("gpsimd", "c_v", c("c_v"))
                if ti >= 2:
                    wait("gpsimd", f"s_xqs{par}", 16 * (ti // 2))
                E.dma("gpsimd", xq_s[ti * 128:(ti + 1) * 128, :], xqst,
                      f"s_xqs{par}")

    wsum_qc = min(1, QC - 1)

    def emit_wsum_once():
        if not _wsum_done[0]:
            _wsum_done[0] = True
            emit_wsum()
    for qc in range(QC):
        for p in range(8):
            base_st = c("c_st")
            base_exp = c("c_exp")
            base_pv = c("c_pv")

            def emit_st(kc):
                b3 = (0, 2, 6)[kc % 3]
                wait("tensor", "c_exp", base_exp + kc - 2)
                if b3 == 6:
                    wait("tensor", "c_wsc", c("c_wsc"))
                    wait("tensor", "c_radd", c("c_radd"))
                em("tensor", "matmul", bank(b3),
                   A3[0:64, p * 2048 + kc * 128: p * 2048 + kc * 128 + 128],
                   A2[0:64, p * 2048 + qc * 512: p * 2048 + qc * 512 + 512],
                   start=True, stop=True, tile_position=(0, 0))
                em("tensor", "matmul", bank(b3 + 1),
                   A3[64:128, p * 2048 + kc * 128: p * 2048 + kc * 128 + 128],
                   A2[64:128, p * 2048 + qc * 512: p * 2048 + qc * 512 + 512],
                   start=True, stop=True, tile_position=(64, 0), inc=("c_st", 1))

            emit_st(0)
            emit_st(1)
            emit_st(2)
            for kc in range(T):
                par = kc % 2
                sb3 = (0, 1024, 3072)[kc % 3]
                wait("scalar", "c_st", base_st + kc + 1)
                wait("scalar", "c_pv", base_pv + kc - 1)
                em("scalar", "activation", A7[:, par * 1024:(par + 1) * 1024],
                   ps[:, sb3:sb3 + 1024], AF.Exp, bias=cst[:, 0:1],
                   inc=("c_exp", 1))
                wait("tensor", "c_exp", base_exp + kc + 1)
                if kc == 0:
                    wait("tensor", "c_pve", c("c_pve"))
                if kc == 0 and p == 0:
                    wait("tensor", "c_radd", c("c_radd"))
                va = A4[:, kc * 1040:(kc + 1) * 1040]
                em("tensor", "matmul", ps[0:65, 2048:2560],
                   va[:, 65 * 2 * p: 65 * 2 * p + 65],
                   A7[:, par * 1024: par * 1024 + 512],
                   start=(kc == 0), stop=(kc == T - 1))
                em("tensor", "matmul", ps[0:65, 2560:3072],
                   va[:, 65 * (2 * p + 1): 65 * (2 * p + 1) + 65],
                   A7[:, par * 1024 + 512: par * 1024 + 1024],
                   start=(kc == 0), stop=(kc == T - 1), inc=("c_pv", 1))
                if kc + 3 < T:
                    emit_st(kc + 3)
            wait("scalar", "c_pv", c("c_pv"))
            if qc == 0 and p == 0:
                wait("scalar", "c_xa", T)
                wait("scalar", "s_xts2", 16 * ((T + 1) // 4))
                wait("scalar", "s_xts3", 16 * (T // 4))
            gpi = 8 * qc + p
            row = 64 + 32 * (gpi % 2)
            if attn_lvl >= 2 and gpi >= 2:
                wait("scalar", f"s_str{gpi % 2}", 32 * (gpi // 2))
            if debug and qc >= 1:
                wait("scalar", "s_dbgA", c("s_dbgA"))
            arp = A8[:, p * 512:(p + 1) * 512]
            em("scalar", "copy", arp[0:64, :], ps[0:64, 2048:2560])
            em("scalar", "copy", arp[64:128, :], ps[0:64, 2560:3072])
            em("scalar", "copy",
               A9[row:row + 1, 0:512].rearrange("r (q tc) -> r tc q", tc=4),
               ps[64:65, 2048:2560])
            em("scalar", "copy",
               A9[row:row + 1, 512:1024].rearrange("r (q tc) -> r tc q", tc=4),
               ps[64:65, 2560:3072], inc=("c_pve", 1))
            if attn_lvl < 2:
                continue
            wait("scalar", "c_pve", c("c_pve"))
            for hh in range(2):
                E.dma("scalar",
                      A10[:, 8 * p + 4 * hh: 8 * p + 4 * hh + 4],
                      A9[row:row + 1, hh * 512:(hh + 1) * 512]
                      .rearrange("r (q tc) -> r q tc", tc=4),
                      f"s_str{gpi % 2}")
            if qc == wsum_qc and p == 2:
                emit_wsum_once()
            if _wsum_done[0] and p < 7:
                drain_wq(7)
            if qc >= 1 and p < 7:
                drain_wt(4)
            if qc == 0 and p == 2 and attn_lvl in (3, 4):
                emit_xre_reload(0)
                emit_xre_reload(1)
        if attn_lvl < 2:
            continue
        wait("vector", "s_str0", 128 * (qc + 1))
        wait("vector", "s_str1", 128 * (qc + 1))
        if debug:
            wait("vector", "s_dbgA", c("s_dbgA"))
        em("vector", "reciprocal", A10[:, 64:128], A10[:, 0:64], inc=("c_rcp", 1))

        if debug and qc == 0:
            wait("sync", "c_pve", c("c_pve"))
            dma("sync", dbg_ar[:, :], A8[:, :], "s_dbgA")

        if attn_lvl < 3 and attn_lvl not in (6, 7):
            continue
        for tc in range(4):
            ti = qc * 4 + tc
            par = ti % 2
            xre = A0[:, par * 1024:(par + 1) * 1024]
            wait("tensor", "c_rcp", c("c_rcp"))
            wait("tensor", "c_exp", c("c_exp"))
            base_radd = c("c_radd")
            for g in range(8):
                gp = g % 2
                bA = ps[:, 3584:3712] if gp == 0 else ps[:, 3200:3328]
                bB = ps[:, 2560:2688] if gp == 0 else ps[:, 2048:2176]
                wait("tensor", "c_radd", base_radd + 2 * (g - 1))
                em("tensor", "matmul", bA,
                   A8[0:64, g * 512 + tc * 128: g * 512 + tc * 128 + 128],
                   wproj_sb[0:64, g * 128:(g + 1) * 128],
                   start=True, stop=True, tile_position=(0, 0))
                em("tensor", "matmul", bB,
                   A8[64:128, g * 512 + tc * 128: g * 512 + tc * 128 + 128],
                   wproj_sb[64:128, g * 128:(g + 1) * 128],
                   start=True, stop=True, tile_position=(64, 0), inc=("m_prj", 1))
                if attn_lvl in (6, 7):
                    continue
                bslc = PROJ_B[tc][:, g * 128:(g + 1) * 128]
                wait("vector", "m_prj", c("m_prj"))
                wait("vector", "c_rcp", c("c_rcp"))
                wait("vector", f"s_xr{par}", 16 * (ti // 2 + 1))
                if tc == 0 and g == 0 and qc >= 1:
                    wait("vector", "s_xps0", 16 * 2 * qc)
                    wait("vector", "s_xps1", 16 * 2 * qc)
                if tc == 0 and g == 0 and qc == 0:
                    for nm in ("s_wqv0", "s_wqv1", "s_wpv0", "s_wpv1"):
                        wait("vector", nm, c(nm))
                em("vector", "scalar_tensor_tensor",
                   bslc, bA,
                   A10[:, 64 + 8 * g + tc: 65 + 8 * g + tc],
                   xre[:, g * 128:(g + 1) * 128], ALU.mult, ALU.add,
                   inc=("c_radd", 1))
                wait("vector", "c_radd", c("c_radd"))
                em("vector", "scalar_tensor_tensor",
                   bslc, bB,
                   A10[:, 68 + 8 * g + tc: 69 + 8 * g + tc],
                   bslc, ALU.mult, ALU.add,
                   inc=("c_radd", 1))
            if attn_lvl >= 3 and ti + 2 < 4 * QC:
                emit_xre_reload(ti + 2)
            if debug and qc == 0 and tc == 0:
                wait("sync", "c_radd", c("c_radd"))
                dma("sync", dbg_rc[:, :], A10[:, :], "s_dbgA")
            if attn_lvl != 4:
                continue
            if tc == 3:
                emit_ln_quad(qc)
                if qc < QC - 1:
                    emit_xqt_chunk(qc)
    if debug:
        wait("sync", "e_qkv", c("e_qkv"))
        dma("sync", dbg_qp[:, 0:8 * N],
            A2.rearrange("p (pp n) -> p pp n", pp=8)[:, :, 0:N], "s_dbg")
        dma("sync", dbg_kp[:, 0:8 * N],
            A3.rearrange("p (pp n) -> p pp n", pp=8)[:, :, 0:N], "s_dbg")
        wait("sync", "e_v", c("e_v"))
        dma("sync", dbg_va[:, :], A4[:, 0:T * 1040], "s_dbg")
        wait("sync", "c_wsc", 1)
        dma("sync", dbg_wsc[:, 0:6], wsc[:, 0:6], "s_dbg")

    drain_wq(96)
    drain_wt(32)
    while _wt_pend:
        _wt_store_flush()
    if attn_lvl != 4:
        return finish()

    wait("vector", "c_wsc", 1)
    wait("vector", "c_v", c("c_v"))
    em("vector", "tensor_scalar", scg[:, 0:T], a_sb[:, 0:T], wsc[:, 3:4], None, ALU.mult,
       inc=("c_sc", 1))
    em("vector", "tensor_scalar", scu[:, 0:T], a_sb[:, 0:T], wsc[:, 4:5], None, ALU.mult,
       inc=("c_sc", 1))

    if phase == 2:
        return finish()

    wait("sync", "m_prj", c("m_prj"))
    wait("sync", "c_pv", c("c_pv"))
    wait("sync", "c_pve", c("c_pve"))
    wait("sync", "c_radd", c("c_radd"))
    wait("sync", "c_wq3", c("c_wq3"))
    wait("sync", "c_exp", c("c_exp"))
    wait("sync", "c_st", c("c_st"))
    wait("sync", "s_xr0", c("s_xr0"))
    wait("sync", "s_xr1", c("s_xr1"))
    if debug:
        wait("sync", "s_dbg", c("s_dbg"))
        wait("sync", "s_dbgA", c("s_dbgA"))

    def wu_slice_ap(par, ic):
        if par == 0:
            return wu_sl0[:, ic * 512:(ic + 1) * 512]
        if ic < 4:
            return wu_sl1a[:, ic * 512:(ic + 1) * 512]
        return wu_sl1b[:, (ic - 4) * 512:(ic - 4) * 512 + 512]

    def emit_quant_store(s, t):
        tt = s * ST + t
        wait("vector", "c_h", c("c_h"))
        wait("vector", "c_s", c("c_s"))
        wait("vector", "c_v", c("c_v"))
        em("vector", "tensor_scalar", ln_sb[:, 1:2], A11[:, t:t + 1], 1e-5, None,
           ALU.max, inc=("c_v", 1))
        wait("vector", "c_v", c("c_v"))
        em("vector", "reciprocal", ln_sb[:, 2:3], ln_sb[:, 1:2], inc=("c_v", 1))
        wait("vector", "c_v", c("c_v"))
        em("vector", "tensor_scalar", ln_sb[:, 3:4], ln_sb[:, 2:3], 127.0, None,
           ALU.mult, inc=("c_v", 1))
        em("vector", "tensor_scalar", scd[:, tt:tt + 1], ln_sb[:, 1:2],
           1.0 / 127.0, None, ALU.mult, inc=("c_v", 1))
        wait("vector", "c_wsc", 1)
        wait("vector", "c_v", c("c_v"))
        em("vector", "tensor_scalar", scd[:, tt:tt + 1], scd[:, tt:tt + 1],
           wsc[:, 5:6], None, ALU.mult, inc=("c_v", 1))
        base_s8 = c("c_s")
        base_v8 = c("c_v")
        for q8 in range(8):
            qp = q8 % 2
            sl = slice(q8 * 512, (q8 + 1) * 512)
            half = A12[:, qp * 512:(qp + 1) * 512]
            wait("scalar", "c_v", base_v8 + q8 - 1)
            wait("scalar", "c_s", c("c_s"))
            em("scalar", "activation", half, h_t[t][:, sl], AF.Identity,
               scale=ln_sb[:, 3:4], bias=cst[:, 2:3], inc=("c_s", 1))
            wait("vector", "c_s", base_s8 + q8 + 1)
            em("vector", "tensor_scalar", h_t[t][:, sl], half, MAGIC, None,
               ALU.subtract, inc=("c_v", 1))
        wait("scalar", "c_v", c("c_v"))
        E.dma("scalar", hq_s[tt * 128:(tt + 1) * 128, :], h_t[t][:, :], "s_hqs")

    def emit_gu_load(s, oc):
        par = oc % 2
        gidx = s * 8 + oc
        if gidx >= 2:
            wait("sync", "c_gu", (gidx - 1) * ST, future=True)
        wait("sync", "s_wqs0", 16 * 48, future=True)
        wait("sync", "s_wqs1", 16 * 48, future=True)
        gl = s * 8 + oc
        g4 = gl % 4
        for ic in range(8):
            dma("sync", wg_sl[par][:, ic * 512:(ic + 1) * 512],
                wgq_s[oc * 512:(oc + 1) * 512, ic * 128:(ic + 1) * 128],
                f"s_wgt{g4}", transpose=True)
        for ic in range(8):
            dma("sync", wu_slice_ap(par, ic),
                wuq_s[oc * 512:(oc + 1) * 512, ic * 128:(ic + 1) * 128],
                f"s_wut{g4}", transpose=True)

    def emit_gu(s, preloaded=0):
        for oc in range(8):
            par = oc % 2
            if oc < 2 and preloaded == 0:
                emit_gu_load(s, oc)
            if oc + 2 < 8:
                emit_gu_load(s, oc + 2)
            gl = s * 8 + oc
            g4 = gl % 4
            wait("tensor", f"s_wgt{g4}", 16 * 8 * (gl // 4 + 1), future=True)
            for t in range(ST):
                tt = s * ST + t
                wait("tensor", "s_xqt", 16 * 8 * min(NXC, tt // 4 + 1))
                bg = (t % 2) * 2
                bu = bg + 1
                wait("tensor", "c_gue", max(0, c("c_gue") - 1))
                wait("tensor", "c_h", max(0, c("c_h") - 1))
                for ic in range(8):
                    em("tensor", "matmul", bank(bg),
                       xqT[:, ic * 2048 + tt * 128: ic * 2048 + tt * 128 + 128],
                       wg_sl[par][:, ic * 512:(ic + 1) * 512],
                       start=(ic == 0), stop=(ic == 7))
                if t == 0:
                    wait("tensor", f"s_wut{g4}", 16 * 8 * (gl // 4 + 1),
                         future=True)
                for ic in range(8):
                    em("tensor", "matmul", bank(bu),
                       xqT[:, ic * 2048 + tt * 128: ic * 2048 + tt * 128 + 128],
                       wu_slice_ap(par, ic),
                       start=(ic == 0), stop=(ic == 7),
                       inc=("c_gu", 1) if ic == 7 else None)
                wait("scalar", "c_gu", c("c_gu"))
                wait("scalar", "c_sc", 2)
                wait("scalar", "c_h", c("c_h"))
                wait("scalar", "c_s", c("c_s"))
                if s == 0 and oc == 0 and t == 0:
                    wait("scalar", "s_xps0", 16 * (T // 2))
                    wait("scalar", "s_xps1", 16 * (T // 2))
                em("scalar", "activation", A12[:, 0:512], bank(bg), AF.Silu,
                   scale=scg[:, tt:tt + 1], inc=("c_gue", 1))
                wait("vector", "c_gue", c("c_gue"))
                if s >= 1:
                    wait("vector", "s_hqs", 16 * ST * s)
                em("vector", "scalar_tensor_tensor",
                   h_t[t][:, oc * 512:(oc + 1) * 512], bank(bu),
                   scu[:, tt:tt + 1], A12[:, 0:512], ALU.mult, ALU.mult,
                   inc=("c_h", 1))
                wait("vector", "c_h", c("c_h"))
                if oc == 0:
                    em("vector", "reduce_max", A11[:, t:t + 1],
                       h_t[t][:, 0:512], AX.X, apply_absolute_value=True,
                       inc=("c_v", 1))
                else:
                    em("vector", "reduce_max", A11[:, 8 + t:9 + t],
                       h_t[t][:, oc * 512:(oc + 1) * 512], AX.X,
                       apply_absolute_value=True, inc=("c_v", 1))
                    wait("vector", "c_v", c("c_v"))
                    em("vector", "tensor_max", A11[:, t:t + 1], A11[:, t:t + 1],
                       A11[:, 8 + t:9 + t], inc=("c_v", 1))
                if oc == 7:
                    emit_quant_store(s, t)
            emit_hqt_piece(4)

    _hqt_q = []

    def queue_hqt_half(k):
        for j in range(16):
            _hqt_q.append((k, j))

    def emit_hqt_piece(n):
        for _ in range(n):
            if not _hqt_q:
                return
            k, j = _hqt_q.pop(0)
            s, rem = divmod(k, NTG * 2)
            tg, h = divmod(rem, 2)
            row0 = (s * ST + tg * 4) * 128
            par = k % 2
            wait("sync", "s_hqs", 16 * ST * (s + 1))
            if k >= 2:
                wait("sync", "c_dn", 16 * (k - 1), future=16 * (k - 1) > c("c_dn"))
            jc = h * 16 + j
            dma("sync", hqT[:, par * 8192 + j * 512: par * 8192 + (j + 1) * 512],
                hq_s[row0:row0 + 512, jc * 128:(jc + 1) * 128],
                f"s_hqt{par}", transpose=True)

    def emit_hqt_half(k):
        queue_hqt_half(k)
        emit_hqt_piece(16)

    def emit_down(s):
        for tg in range(NTG):
            gbase = s * NTG + tg
            row0 = (s * ST + tg * 4) * 128
            wait("scalar", "s_xps0", 16 * (T // 2))
            wait("scalar", "s_xps1", 16 * (T // 2))
            if gbase >= 1:
                wait("scalar", "c_r2", gbase * 4, future=True)
                wait("scalar", "c_s", c("c_s"))
                wait("scalar", "s_out0", 16 * 2 * gbase)
                wait("scalar", "s_out1", 16 * 2 * gbase)
            for t4 in range(2):
                E.dma("scalar", xre_f[:, t4 * 1024:(t4 + 1) * 1024],
                      xp_s[row0 + t4 * 128: row0 + (t4 + 1) * 128, :], "s_xre")
            for jc in range(32):
                k = gbase * 2 + jc // 16
                kp = k % 2
                j = jc % 16
                par = jc % 4
                wtile = wdt_sb if par < 2 else wdt2_sb
                po = (par % 2) * 1024
                widx = gbase * 32 + jc
                if widx == 0:
                    wait("sync", "s_wtb0", 16 * 16)
                    wait("sync", "s_wtb1", 16 * 16)
                    wait("sync", "s_xqs0", 16 * (T // 2))
                    wait("sync", "s_xqs1", 16 * (T // 2))
                if widx >= 4:
                    wait("sync", "c_dn", c("c_dn") - 3, future=True)
                dma("sync", wtile[:, po:po + 1024],
                    wdqT_s[jc * 128:(jc + 1) * 128, :], f"s_wdt{par}")
                wait("tensor", f"s_wdt{par}", c(f"s_wdt{par}"))
                wait("tensor", f"s_hqt{kp}", 16 * 16 * (k // 2 + 1))
                if jc == 0:
                    wait("tensor", "c_dne", c("c_dne"))
                    wait("tensor", "c_gue", c("c_gue"))
                    wait("tensor", "c_h", c("c_h"))
                    wait("tensor", "c_r2", c("c_r2"))
                for t4 in range(4):
                    for o2 in range(2):
                        em("tensor", "matmul", bank(t4 * 2 + o2),
                           hqT[:, kp * 8192 + j * 512 + t4 * 128:
                               kp * 8192 + j * 512 + t4 * 128 + 128],
                           wtile[:, po + o2 * 512: po + o2 * 512 + 512],
                           start=(jc == 0), stop=(jc == 31),
                           inc=("c_dn", 1) if (t4 == 3 and o2 == 1) else None)
                if jc == 16 and gbase + 1 < NST * NTG:
                    queue_hqt_half(2 * (gbase + 1))
                emit_hqt_piece(2)
            if gbase + 1 < NST * NTG:
                queue_hqt_half(2 * (gbase + 1) + 1)
            B = [A12[:, 0:1024], A15[:, 0:1024], A15[:, 1024:2048],
                 A15[:, 2048:3072]]
            for t4 in range(4):
                tt = s * ST + tg * 4 + t4
                if t4 == 2:
                    wait("scalar", "c_r2", gbase * 4 + 2, future=True)
                    for t4b in range(2, 4):
                        E.dma("scalar", xre_f[:, (t4b - 2) * 1024:(t4b - 1) * 1024],
                              xp_s[row0 + t4b * 128: row0 + (t4b + 1) * 128, :],
                              "s_xre")
                wait("vector", "c_dn", c("c_dn"))
                wait("vector", "c_v", c("c_v"))
                wait("vector", "c_s", c("c_s"))
                wait("vector", "s_xre", 16 * (gbase * 4 + (t4 // 2) * 2 + 2),
                     future=True)
                xrh = xre_f[:, (t4 % 2) * 1024:(t4 % 2 + 1) * 1024]
                em("vector", "scalar_tensor_tensor", B[t4][:, 0:512],
                   bank(t4 * 2), scd[:, tt:tt + 1], xrh[:, 0:512],
                   ALU.mult, ALU.add, inc=("c_dne", 1))
                em("vector", "scalar_tensor_tensor", B[t4][:, 512:1024],
                   bank(t4 * 2 + 1), scd[:, tt:tt + 1], xrh[:, 512:1024],
                   ALU.mult, ALU.add, inc=("c_r2", 1))
            wait("vector", "c_r2", c("c_r2"))
            for t4 in range(4):
                em("vector", "reduce_sum", ln_sb[:, t4:t4 + 1], B[t4], AX.X,
                   inc=("c_v", 1))
            wait("vector", "c_v", c("c_v"))
            em("vector", "tensor_scalar", ln_sb[:, 4:8], ln_sb[:, 0:4],
               -1.0 / 1024.0, None, ALU.mult, inc=("c_v", 1))
            for t4 in range(4):
                wait("scalar", "c_v", c("c_v"))
                wait("scalar", "c_r2", c("c_r2"))
                em("scalar", "activation", xre_f[:, 0:1024], B[t4], AF.Square,
                   bias=ln_sb[:, 4 + t4:5 + t4], inc=("c_s", 1))
                wait("vector", "c_s", c("c_s"))
                em("vector", "reduce_sum", ln_sb[:, 8 + t4:9 + t4],
                   xre_f[:, 0:1024], AX.X, inc=("c_v", 1))
            wait("vector", "c_v", c("c_v"))
            em("vector", "tensor_scalar", ln_sb[:, 12:16], ln_sb[:, 8:12],
               1.0 / 1024.0, EPS, ALU.mult, ALU.add, inc=("c_v", 1))
            wait("vector", "c_v", c("c_v"))
            em("vector", "tensor_scalar", wt1.bitcast(I32)[:, 4:8],
               ln_sb.bitcast(I32)[:, 12:16], 1, None, ALU.arith_shift_right,
               inc=("c_v", 1))
            wait("vector", "c_v", c("c_v"))
            em("vector", "tensor_scalar", wt1.bitcast(I32)[:, 0:4],
               wt1.bitcast(I32)[:, 4:8], -1, 0x5F3759DF, ALU.mult, ALU.add,
               inc=("c_v", 1))
            for _ in range(4):
                wait("vector", "c_v", c("c_v"))
                em("vector", "tensor_mul", wt1[:, 4:8], wt1[:, 0:4], wt1[:, 0:4],
                   inc=("c_v", 1))
                wait("vector", "c_v", c("c_v"))
                em("vector", "tensor_mul", ln_sb[:, 8:12], wt1[:, 4:8],
                   ln_sb[:, 12:16], inc=("c_v", 1))
                wait("vector", "c_v", c("c_v"))
                em("vector", "tensor_scalar", wt1[:, 4:8], ln_sb[:, 8:12],
                   -0.5, 1.5, ALU.mult, ALU.add, inc=("c_v", 1))
                wait("vector", "c_v", c("c_v"))
                em("vector", "tensor_mul", wt1[:, 0:4], wt1[:, 0:4], wt1[:, 4:8],
                   inc=("c_v", 1))
            wait("vector", "c_v", c("c_v"))
            em("vector", "tensor_mul", ln_sb[:, 8:12], ln_sb[:, 4:8], wt1[:, 0:4],
               inc=("c_v", 1))
            for t4 in range(4):
                tt = s * ST + tg * 4 + t4
                outst = xre_f[:, (t4 % 2) * 1024:(t4 % 2 + 1) * 1024]
                wait("scalar", "c_v", c("c_v"))
                if tt >= 2:
                    wait("scalar", f"s_out{tt % 2}", 16 * (tt // 2))
                em("scalar", "activation", outst, B[t4], AF.Identity,
                   scale=wt1[:, t4:t4 + 1], bias=ln_sb[:, 8 + t4:9 + t4],
                   inc=("c_s", 1))
                wait("scalar", "c_s", c("c_s"))
                E.dma("scalar", y_d[tt * 128:(tt + 1) * 128, :], outst,
                      f"s_out{tt % 2}")

    emit_gu_load(0, 0)
    emit_gu_load(0, 1)
    while _xqt_next[0] < NXC:
        emit_xqt_chunk(_xqt_next[0])
    emit_gu(0, preloaded=2)
    if NST > 1:
        emit_gu_load(1, 0)
        emit_gu_load(1, 1)
        queue_hqt_half(0)
        queue_hqt_half(1)
        emit_gu(1, preloaded=2)
        emit_hqt_piece(32)
    else:
        emit_hqt_half(0)
        emit_hqt_half(1)
    emit_down(0)
    if NST > 1:
        emit_down(1)
    emit_hqt_piece(64)

    wait("sync", "s_out0", 16 * (T // 2))
    wait("sync", "s_out1", 16 * (T // 2))
    if debug:
        wait("sync", "s_dbg", c("s_dbg"))
        wait("sync", "s_dbgA", c("s_dbgA"))

    E.check_future()
    E.replay()
    ctx.close()
    return nc


def _consts():
    cc = np.zeros((128, 8), np.float32)
    cc[:, 0] = -C0
    cc[:, 1] = 0.125
    cc[:, 2] = MAGIC
    return cc


def kernel(L0_attn_x, L0_attn_w_qkv, L0_attn_w_proj, L0_attn_b_proj,
           L0_ffn_w_gate, L0_ffn_w_up, L0_ffn_w_down):
    x = np.ascontiguousarray(np.asarray(L0_attn_x, dtype=np.float32))
    wqkv = np.ascontiguousarray(np.asarray(L0_attn_w_qkv, dtype=np.float32))
    wproj = np.ascontiguousarray(np.asarray(L0_attn_w_proj, dtype=np.float32))
    bproj = np.ascontiguousarray(np.asarray(L0_attn_b_proj, dtype=np.float32)).reshape(1, 1024)
    wg = np.ascontiguousarray(np.asarray(L0_ffn_w_gate, dtype=np.float32))
    wu = np.ascontiguousarray(np.asarray(L0_ffn_w_up, dtype=np.float32))
    wd = np.ascontiguousarray(np.asarray(L0_ffn_w_down, dtype=np.float32))
    cst = _consts()

    nc = build(T=16, NST=2, debug=False)
    in_maps = [dict(x=x[cid], wqkv=wqkv, wproj=wproj, bproj=bproj,
                    wg=wg, wu=wu, wd=wd, cst=cst) for cid in range(8)]
    res = run_bass_kernel_spmd(nc, in_maps, core_ids=list(range(8)))
    out = np.stack([r["y"] for r in res.results], axis=0)
    return out.astype(np.float32)


# version 51
# speedup vs baseline: 1.1946x; 1.1946x over previous
import contextlib
import numpy as np

import concourse.bass as bass
import concourse.mybir as mybir
from concourse.bass_utils import run_bass_kernel_spmd

F32 = mybir.dt.float32
BF16 = mybir.dt.bfloat16
I32 = mybir.dt.int32
AF = mybir.ActivationFunctionType
ALU = mybir.AluOpType
AX = mybir.AxisListType

MAGIC = 3.0 * 2.0 ** 22
C0 = 16.0
EPS = 1e-5
BASE = 16512


class Emitter:
    ENGINES = ("sync", "scalar", "vector", "tensor", "gpsimd")

    def __init__(self, nc, ctx):
        self.nc = nc
        self.ops = {e: [] for e in self.ENGINES}
        self.sems = {}
        self.cnt = {}
        self.last_wait = {e: {} for e in self.ENGINES}
        self.future_waits = []
        self.ctx = ctx

    def sem(self, name):
        if name not in self.sems:
            self.sems[name] = self.ctx.enter_context(self.nc.semaphore(name))
            self.cnt[name] = 0
        return self.sems[name]

    def c(self, name):
        self.sem(name)
        return self.cnt[name]

    def emit(self, eng, meth, *args, inc=None, **kw):
        if inc is not None:
            s, n = inc
            self.sem(s)
            self.cnt[s] += n
            self.ops[eng].append((meth, args, kw, (self.sems[s], n)))
            return self.cnt[s]
        self.ops[eng].append((meth, args, kw, None))
        return None

    def wait(self, eng, semname, val, future=False):
        if val <= 0:
            return
        self.sem(semname)
        if future:
            self.future_waits.append((eng, semname, val))
        else:
            assert val <= self.cnt[semname], \
                f"wait {eng} {semname}>={val} but only {self.cnt[semname]} emitted"
        lw = self.last_wait[eng].get(semname, -1)
        if val <= lw:
            return
        self.last_wait[eng][semname] = val
        self.ops[eng].append(("wait_ge", (self.sems[semname], val), {}, None))

    def check_future(self):
        for eng, s, v in self.future_waits:
            assert v <= self.cnt[s], \
                f"future wait {eng} {s}>={v} never reached (final {self.cnt[s]})"

    def dma(self, eng, out, in_, semname, **kw):
        return self.emit(eng, "dma_start", out, in_, inc=(semname, 16), **kw)

    def replay(self):
        with self.nc.Block() as block:
            def mk(name):
                def body(eng):
                    for meth, args, kw, inc in self.ops[name]:
                        if meth == "wait_ge":
                            eng.wait_ge(*args)
                        else:
                            ins = getattr(eng, meth)(*args, **kw)
                            if inc is not None:
                                ins.then_inc(*inc)
                return body
            block.sync(mk("sync"))
            block.scalar(mk("scalar"))
            block.vector(mk("vector"))
            block.tensor(mk("tensor"))
            block.gpsimd(mk("gpsimd"))


def build(T=16, NST=2, debug=False, phase=3, attn_lvl=4):
    assert T % 4 == 0
    N = T * 128
    QC = N // 512
    ST = T // NST
    assert ST % 4 == 0
    NTG = ST // 4

    nc = bass.Bass("TRN2", target_bir_lowering=False)

    x_d = nc.dram_tensor("x", [N, 1024], F32, kind="ExternalInput")
    wqkv_d = nc.dram_tensor("wqkv", [8, 128, 384], F32, kind="ExternalInput")
    wproj_d = nc.dram_tensor("wproj", [8, 128, 128], F32, kind="ExternalInput")
    bproj_d = nc.dram_tensor("bproj", [1, 1024], F32, kind="ExternalInput")
    wg_d = nc.dram_tensor("wg", [4096, 1024], F32, kind="ExternalInput")
    wu_d = nc.dram_tensor("wu", [4096, 1024], F32, kind="ExternalInput")
    wd_d = nc.dram_tensor("wd", [1024, 4096], F32, kind="ExternalInput")
    cst_d = nc.dram_tensor("cst", [128, 8], F32, kind="ExternalInput")
    y_d = nc.dram_tensor("y", [N, 1024], F32, kind="ExternalOutput")

    sk = "ExternalOutput" if debug else "Internal"
    xbf_s = nc.dram_tensor("xbf_s", [N, 1024], BF16, kind=sk)
    xt_s = nc.dram_tensor("xt_s", [N, 1024], F32, kind=sk)
    xp_s = nc.dram_tensor("xp_s", [N, 1024], F32, kind=sk)
    xq_s = nc.dram_tensor("xq_s", [N, 1024], BF16, kind=sk)
    wgq_s = nc.dram_tensor("wgq_s", [4096, 1024], BF16, kind=sk)
    wuq_s = nc.dram_tensor("wuq_s", [4096, 1024], BF16, kind=sk)
    wdq_s = nc.dram_tensor("wdq_s", [1024, 4096], BF16, kind=sk)
    wgqT_s = nc.dram_tensor("wgqT_s", [1024, 4096], BF16, kind="Internal")
    wuqT_s = nc.dram_tensor("wuqT_s", [1024, 4096], BF16, kind="Internal")
    wdqT_s = nc.dram_tensor("wdqT_s", [4096, 1024], BF16, kind="Internal")
    hq_s = nc.dram_tensor("hq_s", [N, 4096], BF16, kind=sk)
    if debug:
        dbg_qp = nc.dram_tensor("dbg_qp", [128, 8 * 2048], BF16, kind="ExternalOutput")
        dbg_kp = nc.dram_tensor("dbg_kp", [128, 8 * 2048], BF16, kind="ExternalOutput")
        dbg_va = nc.dram_tensor("dbg_va", [128, T * 1040], BF16, kind="ExternalOutput")
        dbg_ar = nc.dram_tensor("dbg_ar", [128, 4096], BF16, kind="ExternalOutput")
        dbg_rc = nc.dram_tensor("dbg_rc", [128, 128], F32, kind="ExternalOutput")
        dbg_wsc = nc.dram_tensor("dbg_wsc", [128, 8], F32, kind="ExternalOutput")

    off = [BASE]

    def alloc(name, shape, dtype, at=None):
        nbytes = int(np.prod(shape[1:])) * mybir.dt.size(dtype)
        nbytes = (nbytes + 31) // 32 * 32
        if at is None:
            at = off[0]
            off[0] = at + nbytes
        assert at + nbytes <= 229344, f"SBUF overflow: {name}"
        return nc.alloc_sbuf_tensor_at(name, list(shape), dtype, offset=at), at

    wqkv_sb, _ = alloc("wqkv_sb", [128, 3072], BF16)
    wproj_sb, _ = alloc("wproj_sb", [128, 1024], BF16)
    bpb, bpbo = alloc("bpb", [128, 1024], F32)
    wtstage, _ = alloc("wtstage", [128, 2048], BF16, at=bpbo)
    cst, _ = alloc("cst_sb", [128, 8], F32)
    onescol, _ = alloc("onescol", [128, 1], F32)
    onesrow = None
    a_sb, _ = alloc("a_sb", [128, 16], F32)
    ln_sb, _ = alloc("ln_sb", [128, 16], F32)
    wsc, _ = alloc("wsc", [128, 8], F32)
    scg, _ = alloc("scg", [128, 16], F32)
    scu, _ = alloc("scu", [128, 16], F32)
    scd, _ = alloc("scd", [128, 16], F32)
    wt1, _ = alloc("wt1", [128, 8], F32)

    A0, A0o = alloc("A0", [128, 2048], F32)
    A1, A1o = alloc("A1", [128, 8 * 2048], BF16)
    A2, A2o = alloc("A2", [128, 8 * 2048], BF16)
    A3, A3o = alloc("A3", [128, 8 * 2048], BF16)
    A4, A4o = alloc("A4", [128, T * 1040], BF16)
    A5, A5o = alloc("A5", [128, 2048], F32)
    A6, A6o = alloc("A6", [128, 2048], BF16)
    A7, A7o = alloc("A7", [128, 2048], BF16)
    A8, A8o = alloc("A8", [128, 4096], BF16)
    A9, A9o = alloc("A9", [128, 1024], F32)
    A10, _ = alloc("A10", [128, 128], F32)
    A11, A11o = alloc("A11", [128, 512], F32)
    A12, _ = alloc("A12", [128, 1024], F32)
    A13, A13o = alloc("A13", [128, 2048], F32)
    A14, A14o = alloc("A14", [128, 2048], BF16)
    wdt2_sb, _ = alloc("wdt2_sb", [128, 2048], BF16, at=A14o)
    A15, A15o = alloc("A15", [128, 3072], F32)
    onesrow, _ = alloc("onesrow", [1, 128], F32, at=A11o)

    wg_sl = [alloc("wg_sl0", [128, 4096], BF16, at=A0o)[0],
             alloc("wg_sl1", [128, 4096], BF16, at=A8o)[0]]
    wu_sl0, _ = alloc("wu_sl0", [128, 4096], BF16, at=A5o)
    wu_sl1a, _ = alloc("wu_sl1a", [128, 2048], BF16, at=A6o)
    wu_sl1b, _ = alloc("wu_sl1b", [128, 2048], BF16, at=A7o)
    wdt_sb, _ = alloc("wdt_sb", [128, 2048], BF16, at=A9o)
    xre_f, _ = alloc("xre_f", [128, 2048], F32, at=A13o)
    h_t = []
    for t in range(8):
        base_off = A3o if t < 4 else A4o
        hh, _ = alloc(f"h_{t}", [128, 4096], BF16, at=base_off + (t % 4) * 8192)
        h_t.append(hh)
    hqT, _ = alloc("hqT", [128, 16384], BF16, at=A2o)
    xqT, _ = alloc("xqT", [128, 16384], BF16, at=A1o)

    assert off[0] <= 229344, f"SBUF static overflow {off[0]}"

    ctx = contextlib.ExitStack()
    ps = ctx.enter_context(nc.psum_tensor("ps", [128, 4096], F32))

    def bank(i, lo=0, hi=512):
        return ps[:, 512 * i + lo: 512 * i + hi]

    PROJ_B = [A15[:, 0:1024], A15[:, 1024:2048], A15[:, 2048:3072],
              A12[:, 0:1024]]

    E = Emitter(nc, ctx)
    em, wait, dma, c = E.emit, E.wait, E.dma, E.c

    def finish():
        for name in sorted(E.cnt):
            if name.startswith("s_") and E.cnt[name] > 0:
                wait("sync", name, E.cnt[name])
        E.replay()
        ctx.close()
        return nc


    dma("scalar", cst[:, :], cst_d[:, :], "s_cst")
    n = dma("scalar", A12[0:1, 0:1024], bproj_d[:, :], "s_bp")

    def emit_w2(j):
        for g in (2 * j, 2 * j + 1):
            if g < 8:
                par = g % 2
                st_ap = A13[:, par * 1024: par * 1024 + 384]
                if g >= 2:
                    wait("scalar", "c_wqc", g - 1)
                nn = dma("scalar", st_ap, wqkv_d[g], f"s_wqv{par}")
                wait("scalar", f"s_wqv{par}", nn)
                em("scalar", "copy", wqkv_sb[:, g * 384:(g + 1) * 384], st_ap,
                   inc=("c_wqc", 1))
            else:
                gp = g - 8
                par = gp % 2
                st_ap = A13[:, par * 1024: par * 1024 + 128]
                if gp < 2:
                    wait("scalar", "c_wqc", 8)
                else:
                    wait("scalar", "c_wpc", gp - 1)
                nn = dma("scalar", st_ap, wproj_d[gp], f"s_wpv{par}")
                wait("scalar", f"s_wpv{par}", nn)
                em("scalar", "copy", wproj_sb[:, gp * 128:(gp + 1) * 128], st_ap,
                   inc=("c_wpc", 1))

    em("vector", "memset", onescol[:, :], 1.0, inc=("c_ones", 1))
    em("vector", "memset", onesrow[:, :], 1.0, inc=("c_ones", 1))
    for t in range(T):
        va = A4[:, t * 1040:(t + 1) * 1040]
        em("vector", "memset",
           va.rearrange("p (h d) -> p h d", h=16)[:, :, 64:65], 1.0, inc=("c_vone", 1))

    wait("tensor", "s_bp", 16)
    wait("tensor", "c_ones", 2)
    em("tensor", "matmul", bank(6), onesrow[0:1, :], A12[0:1, 0:512],
       start=True, stop=True)
    em("tensor", "matmul", bank(7), onesrow[0:1, :], A12[0:1, 512:1024],
       start=True, stop=True, inc=("m_bpb", 1))
    wait("scalar", "m_bpb", 1)
    em("scalar", "copy", bpb[:, 0:512], bank(6))
    em("scalar", "copy", bpb[:, 512:1024], bank(7), inc=("c_bpb", 1))

    def xslot(i):
        p4 = i % 4
        if p4 < 2:
            return A0[:, p4 * 1024:(p4 + 1) * 1024]
        return A8.bitcast(F32)[:, (p4 - 2) * 1024:(p4 - 1) * 1024]

    NXC = max(1, N // 512)
    _w2_next = [0]
    _chunk_at = {}
    for k2 in range(NXC):
        _chunk_at.setdefault(min(T - 1, 4 * k2 + 4), []).append(k2)
    for i in range(T):
        p4 = i % 4
        par = i % 2
        xst = xslot(i)
        xbst = A14[:, par * 1024:(par + 1) * 1024]
        if i >= 4:
            wait("sync", f"s_xts{p4}", 16 * (i // 4))
        nn = dma("sync", xst, x_d[i * 128:(i + 1) * 128, :], f"s_x{p4}")
        wait("scalar", f"s_x{p4}", nn)
        wait("scalar", "c_bpb", 1)
        if i >= 2:
            wait("scalar", f"s_xbf{par}", 16 * (i // 2))
        em("scalar", "copy", xbst, xst, inc=("c_xc", 1))
        wait("vector", f"s_x{p4}", nn)
        wait("vector", "c_bpb", 1)
        wait("vector", "c_xc", i + 1)
        em("vector", "tensor_add", xst, xst, bpb[:, :], inc=("c_xa", 1))
        wait("scalar", "c_xc", i + 1)
        dma("scalar", xbf_s[i * 128:(i + 1) * 128, :], xbst, f"s_xbf{par}")
        wait("gpsimd", "c_xa", i + 1)
        dma("gpsimd", xt_s[i * 128:(i + 1) * 128, :], xst, f"s_xts{p4}")
        if i >= 1 and _w2_next[0] < 8:
            emit_w2(_w2_next[0])
            _w2_next[0] += 1
        if i == T - 1:
            while _w2_next[0] < 8:
                emit_w2(_w2_next[0])
                _w2_next[0] += 1
        for k2 in _chunk_at.get(i, []):
            need = min(T, 4 * (k2 + 1))
            wait("sync", "s_xbf0", 16 * ((need + 1) // 2))
            wait("sync", "s_xbf1", 16 * (need // 2))
            r0, r1 = k2 * 512, min(N, k2 * 512 + 512)
            for g in range(8):
                if k2:
                    wait("sync", f"s_xt{g}", 16 * k2)
                dma("sync", A1[:, g * 2048 + r0: g * 2048 + r1],
                    xbf_s[r0:r1, g * 128:(g + 1) * 128], f"s_xt{g}",
                    transpose=True)

    def w_tiles(w):
        R, Cc = w.shape
        return [w[r * 128:(r + 1) * 128, cc * 1024:(cc + 1) * 1024]
                for r in range(R // 128) for cc in range(Cc // 1024)]

    wmats = [wg_d, wu_d, wd_d]
    acc_slot = [wt1[:, 0:1], wt1[:, 1:2], wt1[:, 2:3]]
    part_slot = wt1[:, 3:4]

    def emit_wmean():
        def stg_ap(par):
            if par < 2:
                return A5[:, par * 1024:(par + 1) * 1024]
            if par < 4:
                return A13[:, (par - 2) * 1024:(par - 1) * 1024]
            return PROJ_B[par - 4]
        for mi, w_d2 in enumerate(wmats):
            for ci, tl in enumerate(w_tiles(w_d2)):
                gidx = mi * 32 + ci
                par = gidx % 8
                stg = stg_ap(par)
                if gidx < 8:
                    wait("sync", "c_wpc", 8)
                if gidx >= 8 and gidx % 4 == 0:
                    wait("sync", "c_wma", gidx - 4, future=True)
                sm = ("s_wm0", "s_wm1", "s_wm2", "s_wm3", "s_wqv0",
                      "s_wqv1", "s_wpv0", "s_wpv1")[par]
                nn = dma("sync", stg, tl, sm)
                wait("vector", sm, nn)
                wait("vector", "c_wma", c("c_wma"))
                em("vector", "reduce_sum", part_slot, stg, AX.X,
                   apply_absolute_value=True, inc=("c_wmr", 1))
                wait("vector", "c_wmr", c("c_wmr"))
                if ci == 0:
                    em("vector", "tensor_copy", acc_slot[mi], part_slot,
                       inc=("c_wma", 1))
                else:
                    em("vector", "tensor_add", acc_slot[mi], acc_slot[mi],
                       part_slot, inc=("c_wma", 1))

    def emit_wsum():
        wait("tensor", "c_bpb", 1)
        wait("tensor", "c_exp", c("c_exp"))
        wait("tensor", "c_wma", c("c_wma"))
        for mi in range(3):
            em("tensor", "matmul", ps[0:1, 3136 + mi:3137 + mi], onescol[:, 0:1],
               acc_slot[mi], start=True, stop=True, inc=("m_wsum", 1))
        wait("scalar", "m_wsum", 3)
        em("scalar", "copy", wt1[0:1, 4:7], ps[0:1, 3136:3139], inc=("c_wse", 1))
        wait("vector", "c_wse", 1)
        em("vector", "tensor_scalar", wt1[0:1, 4:7], wt1[0:1, 4:7],
           1.0 / (4096.0 * 1024.0), 1e-5, ALU.mult, ALU.max, inc=("c_wv", 1))
        wait("vector", "c_wv", 1)
        em("vector", "reciprocal", wt1[0:1, 0:3], wt1[0:1, 4:7], inc=("c_wv", 1))
        wait("tensor", "c_wv", 2)
        em("tensor", "matmul", ps[:, 3139:3142], onesrow[0:1, :], wt1[0:1, 0:3],
           start=True, stop=True)
        em("tensor", "matmul", ps[:, 3142:3145], onesrow[0:1, :], wt1[0:1, 4:7],
           start=True, stop=True, inc=("m_wbc", 1))
        wait("scalar", "m_wbc", 1)
        em("scalar", "copy", wsc[:, 0:3], ps[:, 3139:3142])
        em("scalar", "copy", wsc[:, 3:6], ps[:, 3142:3145], inc=("c_wsc", 1))

    wq_scr = [wgq_s, wuq_s, wdq_s]
    _wq_units = []
    for mi in (2, 0, 1):
        for tii, to in zip(w_tiles(wmats[mi]), w_tiles(wq_scr[mi])):
            _wq_units.append((mi, tii, to))
    _wq_emitted = [0]

    def emit_wq_load(gidx):
        mi, tii, to = _wq_units[gidx]
        par = gidx % 2
        stg = A5[:, par * 1024:(par + 1) * 1024]
        if gidx >= 2:
            wait("sync", "c_wq3", 3 * (gidx - 1), future=True)
        else:
            wait("sync", "c_wma", 96, future=True)
        dma("sync", stg, tii, f"s_wqp{par}")

    def drain_wq(k):
        for _ in range(k):
            gidx = _wq_emitted[0]
            if gidx >= len(_wq_units):
                return
            _wq_emitted[0] = gidx + 1
            mi, tii, to = _wq_units[gidx]
            par = gidx % 2
            stg = A5[:, par * 1024:(par + 1) * 1024]
            obuf = A6[:, par * 1024:(par + 1) * 1024]
            if gidx == 0:
                emit_wq_load(0)
                emit_wq_load(1)
            if gidx + 2 < len(_wq_units):
                emit_wq_load(gidx + 2)
            wait("vector", f"s_wqp{par}", 16 * (gidx // 2 + 1))
            wait("vector", "c_wsc", 1)
            wait("vector", "c_wq3", c("c_wq3"))
            em("vector", "tensor_scalar", stg, stg, wsc[:, mi:mi + 1], 1.0,
               ALU.mult, ALU.min, inc=("c_wq3", 1))
            wait("vector", "c_wq3", c("c_wq3"))
            em("vector", "tensor_scalar", stg, stg, -1.0, MAGIC, ALU.max, ALU.add,
               inc=("c_wq3", 1))
            wait("vector", "c_wq3", c("c_wq3"))
            if gidx >= 2:
                wait("vector", f"s_wqs{par}", 16 * (gidx // 2))
            em("vector", "tensor_scalar", obuf, stg, MAGIC, None, ALU.subtract,
               inc=("c_wq3", 1))
            wait("sync", "c_wq3", c("c_wq3"), future=True)
            dma("sync", to, obuf, f"s_wqs{par}")

    emit_wmean()

    if phase == 0:
        return finish()

    _wsum_done = [False]

    _wt_units = list(range(32))
    _wt_issue = [0]
    _wt_pend = []

    def _wt_store_flush():
        if _wt_pend:
            idx2, ic3 = _wt_pend.pop(0)
            par2 = idx2 % 2
            buf2 = wtstage[:, par2 * 1024:(par2 + 1) * 1024]
            wait("gpsimd", f"s_wtt{par2}", 16 * (idx2 // 2 + 1))
            dma("gpsimd", wdqT_s[ic3 * 128:(ic3 + 1) * 128, :], buf2,
                f"s_wtb{par2}")

    def drain_wt(k):
        for _ in range(k):
            if not _wt_units:
                return
            if _wq_emitted[0] < 32:
                return
            ic2 = _wt_units.pop(0)
            idx = _wt_issue[0]
            _wt_issue[0] = idx + 1
            par = idx % 2
            buf = wtstage[:, par * 1024:(par + 1) * 1024]
            wait("sync", "c_xa", T)
            wait("sync", "s_wqs0", 16 * 16, future=True)
            wait("sync", "s_wqs1", 16 * 16, future=True)
            if idx >= 2:
                wait("sync", f"s_wtb{par}", 16 * (idx // 2))
            dma("sync", buf, wdq_s[0:1024, ic2 * 128:(ic2 + 1) * 128],
                f"s_wtt{par}", transpose=True)
            _wt_pend.append((idx, ic2))
            if len(_wt_pend) > 1:
                _wt_store_flush()

    _xqt_next = [0]

    def emit_xqt_chunk(qc2):
        _xqt_next[0] = qc2 + 1
        r0, r2 = qc2 * 512, min(N, qc2 * 512 + 512)
        ntile = r2 // 128
        wait("sync", "e_qkv", 16 * QC)
        wait("sync", "e_v", 3 * T)
        wait("sync", "s_xqs0", 16 * ((ntile + 1) // 2))
        wait("sync", "s_xqs1", 16 * (ntile // 2))
        if qc2:
            wait("sync", "s_xqt", 16 * 8 * qc2)
        for g in range(8):
            dma("sync", xqT[:, g * 2048 + r0: g * 2048 + r2],
                xq_s[r0:r2, g * 128:(g + 1) * 128], "s_xqt", transpose=True)

    wait("tensor", "c_wqc", 8)
    for qt in range(QC):
        for cc in range(16):
            ch0 = cc * 128 if cc < 8 else 1024 + (cc - 8) * 128
            g, o = ch0 // 384, ch0 % 384
            dest = A2 if cc < 8 else A3
            p = cc % 8
            idx = qt * 16 + cc
            b = idx % 4
            if idx >= 4:
                wait("tensor", "e_qkv", idx - 3)
            wait("tensor", f"s_xt{g}", 16 * (qt + 1))
            em("tensor", "matmul", bank(b), wqkv_sb[:, g * 384 + o: g * 384 + o + 128],
               A1[:, g * 2048 + qt * 512: g * 2048 + qt * 512 + 512],
               start=True, stop=True, inc=("m_qkv", 1))
            wait("scalar", "m_qkv", c("m_qkv"))
            wait("scalar", "s_cst", 16)
            dap = dest[:, p * 2048 + qt * 512: p * 2048 + qt * 512 + 512]
            if cc < 8:
                em("scalar", "activation", dap, bank(b), AF.Copy, scale=cst[:, 1:2],
                   inc=("e_qkv", 1))
            else:
                em("scalar", "copy", dap, bank(b), inc=("e_qkv", 1))
    wait("tensor", "e_qkv", 16 * QC)
    vgroups = [(5, 128, 256), (6, 0, 384), (7, 0, 384)]
    for t in range(T):
        for gi, (g, o0, nv) in enumerate(vgroups):
            idx = t * 3 + gi
            b = idx % 4
            if idx >= 4:
                wait("tensor", "e_v", idx - 3)
            wait("tensor", f"s_xt{g}", 16 * (t // 4 + 1))
            em("tensor", "matmul", bank(b, 0, nv),
               A1[:, g * 2048 + t * 128: g * 2048 + t * 128 + 128],
               wqkv_sb[:, g * 384 + o0: g * 384 + o0 + nv],
               start=True, stop=True, inc=("m_v", 1))
            vch0 = g * 384 + o0 - 2048
            h0, nh = vch0 // 64, nv // 64
            va = A4[:, t * 1040:(t + 1) * 1040]
            wait("scalar", "m_v", c("m_v"))
            em("scalar", "copy",
               va[:, 65 * h0: 65 * h0 + 65 * nh].rearrange("p (h d) -> p h d", h=nh)[:, :, 0:64],
               bank(b, 0, nv).rearrange("p (h d) -> p h d", h=nh),
               inc=("e_v", 1))

    if phase == 1:
        return finish()

    wait("tensor", "e_v", 3 * T)
    wait("tensor", "c_vone", T)

    def emit_xre_reload(ti):
        par = ti % 2
        if ti < 2:
            wait("scalar", "c_xc", T)
            wait("scalar", "c_xa", T)
            for p4_ in range(4):
                wait("scalar", f"s_xts{p4_}", 16 * ((T + 3 - p4_) // 4))
            wait("scalar", "s_xbf0", 16 * (T // 2))
            wait("scalar", "s_xbf1", 16 * (T // 2))
        else:
            wait("scalar", "c_radd", c("c_radd"))
        E.dma("scalar", A0[:, par * 1024:(par + 1) * 1024],
              xt_s[ti * 128:(ti + 1) * 128, :], f"s_xr{par}")


    def emit_ln_quad(qc):
        t0 = qc * 4
        sq = A13[:, 0:1024]
        wait("vector", "c_radd", c("c_radd"))
        wait("vector", "c_v", c("c_v"))
        for j in range(4):
            em("vector", "reduce_sum", ln_sb[:, j:j + 1], PROJ_B[j], AX.X,
               inc=("c_v", 1))
        wait("vector", "c_v", c("c_v"))
        em("vector", "tensor_scalar", ln_sb[:, 8:12], ln_sb[:, 0:4],
           -1.0 / 1024.0, None, ALU.mult, inc=("c_v", 1))
        for j in range(4):
            wait("vector", "c_v", c("c_v"))
            em("vector", "tensor_scalar", PROJ_B[j], PROJ_B[j],
               ln_sb[:, 8 + j:9 + j], None, ALU.add, inc=("c_v", 1))
            wait("vector", "c_v", c("c_v"))
            em("vector", "tensor_mul", sq, PROJ_B[j], PROJ_B[j],
               inc=("c_v", 1))
            wait("vector", "c_v", c("c_v"))
            em("vector", "reduce_sum", ln_sb[:, 4 + j:5 + j], sq, AX.X,
               inc=("c_v", 1))
        wait("vector", "c_v", c("c_v"))
        em("vector", "tensor_scalar", ln_sb[:, 4:8], ln_sb[:, 4:8],
           1.0 / 1024.0, EPS, ALU.mult, ALU.add, inc=("c_v", 1))
        wait("vector", "c_v", c("c_v"))
        em("vector", "tensor_scalar", scd.bitcast(I32)[:, 0:4],
           ln_sb.bitcast(I32)[:, 4:8], 1, None, ALU.arith_shift_right,
           inc=("c_v", 1))
        wait("vector", "c_v", c("c_v"))
        em("vector", "tensor_scalar", ln_sb.bitcast(I32)[:, 12:16],
           scd.bitcast(I32)[:, 0:4], -1, 0x5F3759DF, ALU.mult, ALU.add,
           inc=("c_v", 1))
        for _ in range(4):
            wait("vector", "c_v", c("c_v"))
            em("vector", "tensor_mul", scd[:, 0:4], ln_sb[:, 12:16],
               ln_sb[:, 12:16], inc=("c_v", 1))
            wait("vector", "c_v", c("c_v"))
            em("vector", "tensor_mul", scd[:, 4:8], scd[:, 0:4],
               ln_sb[:, 4:8], inc=("c_v", 1))
            wait("vector", "c_v", c("c_v"))
            em("vector", "tensor_scalar", scd[:, 0:4], scd[:, 4:8], -0.5, 1.5,
               ALU.mult, ALU.add, inc=("c_v", 1))
            wait("vector", "c_v", c("c_v"))
            em("vector", "tensor_mul", ln_sb[:, 12:16], ln_sb[:, 12:16],
               scd[:, 0:4], inc=("c_v", 1))
        for jp in range(2):
            for j in (2 * jp, 2 * jp + 1):
                ti = t0 + j
                par = ti % 2
                wait("vector", "c_v", c("c_v"))
                em("vector", "tensor_scalar", PROJ_B[j], PROJ_B[j],
                   ln_sb[:, 12 + j:13 + j], None, ALU.mult, inc=("c_v", 1))
                wait("gpsimd", "c_v", c("c_v"))
                if ti >= 2:
                    wait("gpsimd", f"s_xps{par}", 16 * (ti // 2))
                E.dma("gpsimd", xp_s[ti * 128:(ti + 1) * 128, :], PROJ_B[j],
                      f"s_xps{par}")
                wait("vector", "c_v", c("c_v"))
                em("vector", "reduce_max", scg[:, 8 + j:9 + j], PROJ_B[j],
                   AX.X, apply_absolute_value=True, inc=("c_v", 1))
            c0 = 8 + 2 * jp
            wait("vector", "c_v", c("c_v"))
            em("vector", "tensor_scalar", scg[:, 2 * jp:2 * jp + 2],
               scg[:, c0:c0 + 2], 1e-5, None, ALU.max, inc=("c_v", 1))
            wait("vector", "c_v", c("c_v"))
            em("vector", "reciprocal", scg[:, 4 + 2 * jp:6 + 2 * jp],
               scg[:, 2 * jp:2 * jp + 2], inc=("c_v", 1))
            wait("vector", "c_v", c("c_v"))
            em("vector", "tensor_scalar", scu[:, 2 * jp:2 * jp + 2],
               scg[:, 4 + 2 * jp:6 + 2 * jp], 127.0, None, ALU.mult,
               inc=("c_v", 1))
            em("vector", "tensor_scalar", a_sb[:, t0 + 2 * jp:t0 + 2 * jp + 2],
               scg[:, 2 * jp:2 * jp + 2], 1.0 / 127.0, None, ALU.mult,
               inc=("c_v", 1))
            for j in (2 * jp, 2 * jp + 1):
                ti = t0 + j
                par = ti % 2
                xph = A13[:, par * 1024:(par + 1) * 1024]
                xqst = A14[:, par * 1024:(par + 1) * 1024]
                wait("vector", "c_v", c("c_v"))
                em("vector", "tensor_scalar", xph, PROJ_B[j],
                   scu[:, j:j + 1], MAGIC, ALU.mult, ALU.add, inc=("c_v", 1))
                wait("vector", "c_v", c("c_v"))
                if ti >= 2:
                    wait("vector", f"s_xqs{par}", 16 * (ti // 2))
                em("vector", "tensor_scalar", xqst, xph, MAGIC, None,
                   ALU.subtract, inc=("c_v", 1))
                wait("gpsimd", "c_v", c("c_v"))
                if ti >= 2:
                    wait("gpsimd", f"s_xqs{par}", 16 * (ti // 2))
                E.dma("gpsimd", xq_s[ti * 128:(ti + 1) * 128, :], xqst,
                      f"s_xqs{par}")

    wsum_qc = min(1, QC - 1)

    def emit_wsum_once():
        if not _wsum_done[0]:
            _wsum_done[0] = True
            emit_wsum()
    for qc in range(QC):
        for p in range(8):
            base_st = c("c_st")
            base_exp = c("c_exp")
            base_pv = c("c_pv")

            def emit_st(kc):
                b3 = (0, 2, 6)[kc % 3]
                wait("tensor", "c_exp", base_exp + kc - 2)
                if b3 == 6:
                    wait("tensor", "c_wsc", c("c_wsc"))
                    wait("tensor", "c_radd", c("c_radd"))
                em("tensor", "matmul", bank(b3),
                   A3[0:64, p * 2048 + kc * 128: p * 2048 + kc * 128 + 128],
                   A2[0:64, p * 2048 + qc * 512: p * 2048 + qc * 512 + 512],
                   start=True, stop=True, tile_position=(0, 0))
                em("tensor", "matmul", bank(b3 + 1),
                   A3[64:128, p * 2048 + kc * 128: p * 2048 + kc * 128 + 128],
                   A2[64:128, p * 2048 + qc * 512: p * 2048 + qc * 512 + 512],
                   start=True, stop=True, tile_position=(64, 0), inc=("c_st", 1))

            emit_st(0)
            emit_st(1)
            emit_st(2)
            for kc in range(T):
                par = kc % 2
                sb3 = (0, 1024, 3072)[kc % 3]
                wait("scalar", "c_st", base_st + kc + 1)
                wait("scalar", "c_pv", base_pv + kc - 1)
                em("scalar", "activation", A7[:, par * 1024:(par + 1) * 1024],
                   ps[:, sb3:sb3 + 1024], AF.Exp, bias=cst[:, 0:1],
                   inc=("c_exp", 1))
                wait("tensor", "c_exp", base_exp + kc + 1)
                if kc == 0:
                    wait("tensor", "c_pve", c("c_pve"))
                if kc == 0 and p == 0:
                    wait("tensor", "c_radd", c("c_radd"))
                va = A4[:, kc * 1040:(kc + 1) * 1040]
                em("tensor", "matmul", ps[0:65, 2048:2560],
                   va[:, 65 * 2 * p: 65 * 2 * p + 65],
                   A7[:, par * 1024: par * 1024 + 512],
                   start=(kc == 0), stop=(kc == T - 1))
                em("tensor", "matmul", ps[0:65, 2560:3072],
                   va[:, 65 * (2 * p + 1): 65 * (2 * p + 1) + 65],
                   A7[:, par * 1024 + 512: par * 1024 + 1024],
                   start=(kc == 0), stop=(kc == T - 1), inc=("c_pv", 1))
                if kc + 3 < T:
                    emit_st(kc + 3)
            wait("scalar", "c_pv", c("c_pv"))
            if qc == 0 and p == 0:
                wait("scalar", "c_xa", T)
                wait("scalar", "s_xts2", 16 * ((T + 1) // 4))
                wait("scalar", "s_xts3", 16 * (T // 4))
            gpi = 8 * qc + p
            row = 64 + 32 * (gpi % 2)
            if attn_lvl >= 2 and gpi >= 2:
                wait("scalar", f"s_str{gpi % 2}", 32 * (gpi // 2))
            if debug and qc >= 1:
                wait("scalar", "s_dbgA", c("s_dbgA"))
            arp = A8[:, p * 512:(p + 1) * 512]
            em("scalar", "copy", arp[0:64, :], ps[0:64, 2048:2560])
            em("scalar", "copy", arp[64:128, :], ps[0:64, 2560:3072])
            em("scalar", "copy",
               A9[row:row + 1, 0:512].rearrange("r (q tc) -> r tc q", tc=4),
               ps[64:65, 2048:2560])
            em("scalar", "copy",
               A9[row:row + 1, 512:1024].rearrange("r (q tc) -> r tc q", tc=4),
               ps[64:65, 2560:3072], inc=("c_pve", 1))
            if attn_lvl < 2:
                continue
            wait("scalar", "c_pve", c("c_pve"))
            for hh in range(2):
                E.dma("scalar",
                      A10[:, 8 * p + 4 * hh: 8 * p + 4 * hh + 4],
                      A9[row:row + 1, hh * 512:(hh + 1) * 512]
                      .rearrange("r (q tc) -> r q tc", tc=4),
                      f"s_str{gpi % 2}")
            if qc == wsum_qc and p == 2:
                emit_wsum_once()
            if _wsum_done[0] and p < 7:
                drain_wq(6)
            if qc >= 1 and p < 7:
                drain_wt(4)
            if qc == 0 and p == 2 and attn_lvl in (3, 4):
                emit_xre_reload(0)
                emit_xre_reload(1)
        if attn_lvl < 2:
            continue
        wait("vector", "s_str0", 128 * (qc + 1))
        wait("vector", "s_str1", 128 * (qc + 1))
        if debug:
            wait("vector", "s_dbgA", c("s_dbgA"))
        em("vector", "reciprocal", A10[:, 64:128], A10[:, 0:64], inc=("c_rcp", 1))

        if debug and qc == 0:
            wait("sync", "c_pve", c("c_pve"))
            dma("sync", dbg_ar[:, :], A8[:, :], "s_dbgA")

        if attn_lvl < 3 and attn_lvl not in (6, 7):
            continue
        for tc in range(4):
            ti = qc * 4 + tc
            par = ti % 2
            xre = A0[:, par * 1024:(par + 1) * 1024]
            wait("tensor", "c_rcp", c("c_rcp"))
            wait("tensor", "c_exp", c("c_exp"))
            base_radd = c("c_radd")
            for g in range(8):
                gp = g % 2
                bA = ps[:, 3584:3712] if gp == 0 else ps[:, 3200:3328]
                bB = ps[:, 2560:2688] if gp == 0 else ps[:, 2048:2176]
                wait("tensor", "c_radd", base_radd + 2 * (g - 1))
                em("tensor", "matmul", bA,
                   A8[0:64, g * 512 + tc * 128: g * 512 + tc * 128 + 128],
                   wproj_sb[0:64, g * 128:(g + 1) * 128],
                   start=True, stop=True, tile_position=(0, 0))
                em("tensor", "matmul", bB,
                   A8[64:128, g * 512 + tc * 128: g * 512 + tc * 128 + 128],
                   wproj_sb[64:128, g * 128:(g + 1) * 128],
                   start=True, stop=True, tile_position=(64, 0), inc=("m_prj", 1))
                if attn_lvl in (6, 7):
                    continue
                bslc = PROJ_B[tc][:, g * 128:(g + 1) * 128]
                wait("vector", "m_prj", c("m_prj"))
                wait("vector", "c_rcp", c("c_rcp"))
                wait("vector", f"s_xr{par}", 16 * (ti // 2 + 1))
                if tc == 0 and g == 0 and qc >= 1:
                    wait("vector", "s_xps0", 16 * 2 * qc)
                    wait("vector", "s_xps1", 16 * 2 * qc)
                if tc == 0 and g == 0 and qc == 0:
                    for nm in ("s_wqv0", "s_wqv1", "s_wpv0", "s_wpv1"):
                        wait("vector", nm, c(nm))
                em("vector", "scalar_tensor_tensor",
                   bslc, bA,
                   A10[:, 64 + 8 * g + tc: 65 + 8 * g + tc],
                   xre[:, g * 128:(g + 1) * 128], ALU.mult, ALU.add,
                   inc=("c_radd", 1))
                wait("vector", "c_radd", c("c_radd"))
                em("vector", "scalar_tensor_tensor",
                   bslc, bB,
                   A10[:, 68 + 8 * g + tc: 69 + 8 * g + tc],
                   bslc, ALU.mult, ALU.add,
                   inc=("c_radd", 1))
            if attn_lvl >= 3 and ti + 2 < 4 * QC:
                emit_xre_reload(ti + 2)
            if debug and qc == 0 and tc == 0:
                wait("sync", "c_radd", c("c_radd"))
                dma("sync", dbg_rc[:, :], A10[:, :], "s_dbgA")
            if attn_lvl != 4:
                continue
            if tc == 3:
                emit_ln_quad(qc)
                if qc < QC - 1:
                    emit_xqt_chunk(qc)
    if debug:
        wait("sync", "e_qkv", c("e_qkv"))
        dma("sync", dbg_qp[:, 0:8 * N],
            A2.rearrange("p (pp n) -> p pp n", pp=8)[:, :, 0:N], "s_dbg")
        dma("sync", dbg_kp[:, 0:8 * N],
            A3.rearrange("p (pp n) -> p pp n", pp=8)[:, :, 0:N], "s_dbg")
        wait("sync", "e_v", c("e_v"))
        dma("sync", dbg_va[:, :], A4[:, 0:T * 1040], "s_dbg")
        wait("sync", "c_wsc", 1)
        dma("sync", dbg_wsc[:, 0:6], wsc[:, 0:6], "s_dbg")

    drain_wq(96)
    drain_wt(32)
    while _wt_pend:
        _wt_store_flush()
    if attn_lvl != 4:
        return finish()

    wait("vector", "c_wsc", 1)
    wait("vector", "c_v", c("c_v"))
    em("vector", "tensor_scalar", scg[:, 0:T], a_sb[:, 0:T], wsc[:, 3:4], None, ALU.mult,
       inc=("c_sc", 1))
    em("vector", "tensor_scalar", scu[:, 0:T], a_sb[:, 0:T], wsc[:, 4:5], None, ALU.mult,
       inc=("c_sc", 1))

    if phase == 2:
        return finish()

    wait("sync", "m_prj", c("m_prj"))
    wait("sync", "c_pv", c("c_pv"))
    wait("sync", "c_pve", c("c_pve"))
    wait("sync", "c_radd", c("c_radd"))
    wait("sync", "c_wq3", c("c_wq3"))
    wait("sync", "c_exp", c("c_exp"))
    wait("sync", "c_st", c("c_st"))
    wait("sync", "s_xr0", c("s_xr0"))
    wait("sync", "s_xr1", c("s_xr1"))
    if debug:
        wait("sync", "s_dbg", c("s_dbg"))
        wait("sync", "s_dbgA", c("s_dbgA"))

    def wu_slice_ap(par, ic):
        if par == 0:
            return wu_sl0[:, ic * 512:(ic + 1) * 512]
        if ic < 4:
            return wu_sl1a[:, ic * 512:(ic + 1) * 512]
        return wu_sl1b[:, (ic - 4) * 512:(ic - 4) * 512 + 512]

    def emit_quant_store(s, t):
        tt = s * ST + t
        wait("vector", "c_h", c("c_h"))
        wait("vector", "c_s", c("c_s"))
        wait("vector", "c_v", c("c_v"))
        em("vector", "tensor_scalar", ln_sb[:, 1:2], A11[:, t:t + 1], 1e-5, None,
           ALU.max, inc=("c_v", 1))
        wait("vector", "c_v", c("c_v"))
        em("vector", "reciprocal", ln_sb[:, 2:3], ln_sb[:, 1:2], inc=("c_v", 1))
        wait("vector", "c_v", c("c_v"))
        em("vector", "tensor_scalar", ln_sb[:, 3:4], ln_sb[:, 2:3], 127.0, None,
           ALU.mult, inc=("c_v", 1))
        em("vector", "tensor_scalar", scd[:, tt:tt + 1], ln_sb[:, 1:2],
           1.0 / 127.0, None, ALU.mult, inc=("c_v", 1))
        wait("vector", "c_wsc", 1)
        wait("vector", "c_v", c("c_v"))
        em("vector", "tensor_scalar", scd[:, tt:tt + 1], scd[:, tt:tt + 1],
           wsc[:, 5:6], None, ALU.mult, inc=("c_v", 1))
        base_s8 = c("c_s")
        base_v8 = c("c_v")
        for q8 in range(8):
            qp = q8 % 2
            sl = slice(q8 * 512, (q8 + 1) * 512)
            half = A12[:, qp * 512:(qp + 1) * 512]
            wait("scalar", "c_v", base_v8 + q8 - 1)
            wait("scalar", "c_s", c("c_s"))
            em("scalar", "activation", half, h_t[t][:, sl], AF.Identity,
               scale=ln_sb[:, 3:4], bias=cst[:, 2:3], inc=("c_s", 1))
            wait("vector", "c_s", base_s8 + q8 + 1)
            em("vector", "tensor_scalar", h_t[t][:, sl], half, MAGIC, None,
               ALU.subtract, inc=("c_v", 1))
        wait("scalar", "c_v", c("c_v"))
        E.dma("scalar", hq_s[tt * 128:(tt + 1) * 128, :], h_t[t][:, :], "s_hqs")

    def emit_gu_load(s, oc):
        par = oc % 2
        gidx = s * 8 + oc
        if gidx >= 2:
            wait("sync", "c_gu", (gidx - 1) * ST, future=True)
        wait("sync", "s_wqs0", 16 * 48, future=True)
        wait("sync", "s_wqs1", 16 * 48, future=True)
        gl = s * 8 + oc
        g4 = gl % 4
        for ic in range(8):
            dma("sync", wg_sl[par][:, ic * 512:(ic + 1) * 512],
                wgq_s[oc * 512:(oc + 1) * 512, ic * 128:(ic + 1) * 128],
                f"s_wgt{g4}", transpose=True)
        for ic in range(8):
            dma("sync", wu_slice_ap(par, ic),
                wuq_s[oc * 512:(oc + 1) * 512, ic * 128:(ic + 1) * 128],
                f"s_wut{g4}", transpose=True)

    def emit_gu(s, preloaded=0):
        for oc in range(8):
            par = oc % 2
            if oc < 2 and preloaded == 0:
                emit_gu_load(s, oc)
            if oc + 2 < 8:
                emit_gu_load(s, oc + 2)
            gl = s * 8 + oc
            g4 = gl % 4
            wait("tensor", f"s_wgt{g4}", 16 * 8 * (gl // 4 + 1), future=True)
            for t in range(ST):
                tt = s * ST + t
                wait("tensor", "s_xqt", 16 * 8 * min(NXC, tt // 4 + 1))
                bg = (t % 2) * 2
                bu = bg + 1
                wait("tensor", "c_gue", max(0, c("c_gue") - 1))
                wait("tensor", "c_h", max(0, c("c_h") - 1))
                for ic in range(8):
                    em("tensor", "matmul", bank(bg),
                       xqT[:, ic * 2048 + tt * 128: ic * 2048 + tt * 128 + 128],
                       wg_sl[par][:, ic * 512:(ic + 1) * 512],
                       start=(ic == 0), stop=(ic == 7))
                if t == 0:
                    wait("tensor", f"s_wut{g4}", 16 * 8 * (gl // 4 + 1),
                         future=True)
                for ic in range(8):
                    em("tensor", "matmul", bank(bu),
                       xqT[:, ic * 2048 + tt * 128: ic * 2048 + tt * 128 + 128],
                       wu_slice_ap(par, ic),
                       start=(ic == 0), stop=(ic == 7),
                       inc=("c_gu", 1) if ic == 7 else None)
                wait("scalar", "c_gu", c("c_gu"))
                wait("scalar", "c_sc", 2)
                wait("scalar", "c_h", c("c_h"))
                wait("scalar", "c_s", c("c_s"))
                if s == 0 and oc == 0 and t == 0:
                    wait("scalar", "s_xps0", 16 * (T // 2))
                    wait("scalar", "s_xps1", 16 * (T // 2))
                em("scalar", "activation", A12[:, 0:512], bank(bg), AF.Silu,
                   scale=scg[:, tt:tt + 1], inc=("c_gue", 1))
                wait("vector", "c_gue", c("c_gue"))
                if s >= 1:
                    wait("vector", "s_hqs", 16 * ST * s)
                em("vector", "scalar_tensor_tensor",
                   h_t[t][:, oc * 512:(oc + 1) * 512], bank(bu),
                   scu[:, tt:tt + 1], A12[:, 0:512], ALU.mult, ALU.mult,
                   inc=("c_h", 1))
                wait("vector", "c_h", c("c_h"))
                if oc == 0:
                    em("vector", "reduce_max", A11[:, t:t + 1],
                       h_t[t][:, 0:512], AX.X, apply_absolute_value=True,
                       inc=("c_v", 1))
                else:
                    em("vector", "reduce_max", A11[:, 8 + t:9 + t],
                       h_t[t][:, oc * 512:(oc + 1) * 512], AX.X,
                       apply_absolute_value=True, inc=("c_v", 1))
                    wait("vector", "c_v", c("c_v"))
                    em("vector", "tensor_max", A11[:, t:t + 1], A11[:, t:t + 1],
                       A11[:, 8 + t:9 + t], inc=("c_v", 1))
                if oc == 7:
                    emit_quant_store(s, t)
            emit_hqt_piece(4)

    _hqt_q = []

    def queue_hqt_half(k):
        for j in range(16):
            _hqt_q.append((k, j))

    def emit_hqt_piece(n):
        for _ in range(n):
            if not _hqt_q:
                return
            k, j = _hqt_q.pop(0)
            s, rem = divmod(k, NTG * 2)
            tg, h = divmod(rem, 2)
            row0 = (s * ST + tg * 4) * 128
            par = k % 2
            wait("sync", "s_hqs", 16 * ST * (s + 1))
            if k >= 2:
                wait("sync", "c_dn", 16 * (k - 1), future=16 * (k - 1) > c("c_dn"))
            jc = h * 16 + j
            dma("sync", hqT[:, par * 8192 + j * 512: par * 8192 + (j + 1) * 512],
                hq_s[row0:row0 + 512, jc * 128:(jc + 1) * 128],
                f"s_hqt{par}", transpose=True)

    def emit_hqt_half(k):
        queue_hqt_half(k)
        emit_hqt_piece(16)

    def emit_down(s):
        for tg in range(NTG):
            gbase = s * NTG + tg
            row0 = (s * ST + tg * 4) * 128
            wait("scalar", "s_xps0", 16 * (T // 2))
            wait("scalar", "s_xps1", 16 * (T // 2))
            if gbase >= 1:
                wait("scalar", "c_r2", gbase * 4, future=True)
                wait("scalar", "c_s", c("c_s"))
                wait("scalar", "s_out0", 16 * 2 * gbase)
                wait("scalar", "s_out1", 16 * 2 * gbase)
            for t4 in range(2):
                E.dma("scalar", xre_f[:, t4 * 1024:(t4 + 1) * 1024],
                      xp_s[row0 + t4 * 128: row0 + (t4 + 1) * 128, :], "s_xre")
            for jc in range(32):
                k = gbase * 2 + jc // 16
                kp = k % 2
                j = jc % 16
                par = jc % 4
                wtile = wdt_sb if par < 2 else wdt2_sb
                po = (par % 2) * 1024
                widx = gbase * 32 + jc
                if widx == 0:
                    wait("sync", "s_wtb0", 16 * 16)
                    wait("sync", "s_wtb1", 16 * 16)
                    wait("sync", "s_xqs0", 16 * (T // 2))
                    wait("sync", "s_xqs1", 16 * (T // 2))
                if widx >= 4:
                    wait("sync", "c_dn", c("c_dn") - 3, future=True)
                dma("sync", wtile[:, po:po + 1024],
                    wdqT_s[jc * 128:(jc + 1) * 128, :], f"s_wdt{par}")
                wait("tensor", f"s_wdt{par}", c(f"s_wdt{par}"))
                wait("tensor", f"s_hqt{kp}", 16 * 16 * (k // 2 + 1))
                if jc == 0:
                    wait("tensor", "c_dne", c("c_dne"))
                    wait("tensor", "c_gue", c("c_gue"))
                    wait("tensor", "c_h", c("c_h"))
                    wait("tensor", "c_r2", c("c_r2"))
                for t4 in range(4):
                    for o2 in range(2):
                        em("tensor", "matmul", bank(t4 * 2 + o2),
                           hqT[:, kp * 8192 + j * 512 + t4 * 128:
                               kp * 8192 + j * 512 + t4 * 128 + 128],
                           wtile[:, po + o2 * 512: po + o2 * 512 + 512],
                           start=(jc == 0), stop=(jc == 31),
                           inc=("c_dn", 1) if (t4 == 3 and o2 == 1) else None)
                if jc == 16 and gbase + 1 < NST * NTG:
                    queue_hqt_half(2 * (gbase + 1))
                emit_hqt_piece(2)
            if gbase + 1 < NST * NTG:
                queue_hqt_half(2 * (gbase + 1) + 1)
            B = [A12[:, 0:1024], A15[:, 0:1024], A15[:, 1024:2048],
                 A15[:, 2048:3072]]
            for t4 in range(4):
                tt = s * ST + tg * 4 + t4
                if t4 == 2:
                    wait("scalar", "c_r2", gbase * 4 + 2, future=True)
                    for t4b in range(2, 4):
                        E.dma("scalar", xre_f[:, (t4b - 2) * 1024:(t4b - 1) * 1024],
                              xp_s[row0 + t4b * 128: row0 + (t4b + 1) * 128, :],
                              "s_xre")
                wait("vector", "c_dn", c("c_dn"))
                wait("vector", "c_v", c("c_v"))
                wait("vector", "c_s", c("c_s"))
                wait("vector", "s_xre", 16 * (gbase * 4 + (t4 // 2) * 2 + 2),
                     future=True)
                xrh = xre_f[:, (t4 % 2) * 1024:(t4 % 2 + 1) * 1024]
                em("vector", "scalar_tensor_tensor", B[t4][:, 0:512],
                   bank(t4 * 2), scd[:, tt:tt + 1], xrh[:, 0:512],
                   ALU.mult, ALU.add, inc=("c_dne", 1))
                em("vector", "scalar_tensor_tensor", B[t4][:, 512:1024],
                   bank(t4 * 2 + 1), scd[:, tt:tt + 1], xrh[:, 512:1024],
                   ALU.mult, ALU.add, inc=("c_r2", 1))
            wait("vector", "c_r2", c("c_r2"))
            for t4 in range(4):
                em("vector", "reduce_sum", ln_sb[:, t4:t4 + 1], B[t4], AX.X,
                   inc=("c_v", 1))
            wait("vector", "c_v", c("c_v"))
            em("vector", "tensor_scalar", ln_sb[:, 4:8], ln_sb[:, 0:4],
               -1.0 / 1024.0, None, ALU.mult, inc=("c_v", 1))
            for t4 in range(4):
                wait("scalar", "c_v", c("c_v"))
                wait("scalar", "c_r2", c("c_r2"))
                em("scalar", "activation", xre_f[:, 0:1024], B[t4], AF.Square,
                   bias=ln_sb[:, 4 + t4:5 + t4], inc=("c_s", 1))
                wait("vector", "c_s", c("c_s"))
                em("vector", "reduce_sum", ln_sb[:, 8 + t4:9 + t4],
                   xre_f[:, 0:1024], AX.X, inc=("c_v", 1))
            wait("vector", "c_v", c("c_v"))
            em("vector", "tensor_scalar", ln_sb[:, 12:16], ln_sb[:, 8:12],
               1.0 / 1024.0, EPS, ALU.mult, ALU.add, inc=("c_v", 1))
            wait("vector", "c_v", c("c_v"))
            em("vector", "tensor_scalar", wt1.bitcast(I32)[:, 4:8],
               ln_sb.bitcast(I32)[:, 12:16], 1, None, ALU.arith_shift_right,
               inc=("c_v", 1))
            wait("vector", "c_v", c("c_v"))
            em("vector", "tensor_scalar", wt1.bitcast(I32)[:, 0:4],
               wt1.bitcast(I32)[:, 4:8], -1, 0x5F3759DF, ALU.mult, ALU.add,
               inc=("c_v", 1))
            for _ in range(4):
                wait("vector", "c_v", c("c_v"))
                em("vector", "tensor_mul", wt1[:, 4:8], wt1[:, 0:4], wt1[:, 0:4],
                   inc=("c_v", 1))
                wait("vector", "c_v", c("c_v"))
                em("vector", "tensor_mul", ln_sb[:, 8:12], wt1[:, 4:8],
                   ln_sb[:, 12:16], inc=("c_v", 1))
                wait("vector", "c_v", c("c_v"))
                em("vector", "tensor_scalar", wt1[:, 4:8], ln_sb[:, 8:12],
                   -0.5, 1.5, ALU.mult, ALU.add, inc=("c_v", 1))
                wait("vector", "c_v", c("c_v"))
                em("vector", "tensor_mul", wt1[:, 0:4], wt1[:, 0:4], wt1[:, 4:8],
                   inc=("c_v", 1))
            wait("vector", "c_v", c("c_v"))
            em("vector", "tensor_mul", ln_sb[:, 8:12], ln_sb[:, 4:8], wt1[:, 0:4],
               inc=("c_v", 1))
            for t4 in range(4):
                tt = s * ST + tg * 4 + t4
                outst = xre_f[:, (t4 % 2) * 1024:(t4 % 2 + 1) * 1024]
                wait("scalar", "c_v", c("c_v"))
                if tt >= 2:
                    wait("scalar", f"s_out{tt % 2}", 16 * (tt // 2))
                em("scalar", "activation", outst, B[t4], AF.Identity,
                   scale=wt1[:, t4:t4 + 1], bias=ln_sb[:, 8 + t4:9 + t4],
                   inc=("c_s", 1))
                wait("scalar", "c_s", c("c_s"))
                E.dma("scalar", y_d[tt * 128:(tt + 1) * 128, :], outst,
                      f"s_out{tt % 2}")

    emit_gu_load(0, 0)
    emit_gu_load(0, 1)
    while _xqt_next[0] < NXC:
        emit_xqt_chunk(_xqt_next[0])
    emit_gu(0, preloaded=2)
    if NST > 1:
        emit_gu_load(1, 0)
        emit_gu_load(1, 1)
        queue_hqt_half(0)
        queue_hqt_half(1)
        emit_gu(1, preloaded=2)
        emit_hqt_piece(32)
    else:
        emit_hqt_half(0)
        emit_hqt_half(1)
    emit_down(0)
    if NST > 1:
        emit_down(1)
    emit_hqt_piece(64)

    wait("sync", "s_out0", 16 * (T // 2))
    wait("sync", "s_out1", 16 * (T // 2))
    if debug:
        wait("sync", "s_dbg", c("s_dbg"))
        wait("sync", "s_dbgA", c("s_dbgA"))

    E.check_future()
    E.replay()
    ctx.close()
    return nc


def _consts():
    cc = np.zeros((128, 8), np.float32)
    cc[:, 0] = -C0
    cc[:, 1] = 0.125
    cc[:, 2] = MAGIC
    return cc


def kernel(L0_attn_x, L0_attn_w_qkv, L0_attn_w_proj, L0_attn_b_proj,
           L0_ffn_w_gate, L0_ffn_w_up, L0_ffn_w_down):
    x = np.ascontiguousarray(np.asarray(L0_attn_x, dtype=np.float32))
    wqkv = np.ascontiguousarray(np.asarray(L0_attn_w_qkv, dtype=np.float32))
    wproj = np.ascontiguousarray(np.asarray(L0_attn_w_proj, dtype=np.float32))
    bproj = np.ascontiguousarray(np.asarray(L0_attn_b_proj, dtype=np.float32)).reshape(1, 1024)
    wg = np.ascontiguousarray(np.asarray(L0_ffn_w_gate, dtype=np.float32))
    wu = np.ascontiguousarray(np.asarray(L0_ffn_w_up, dtype=np.float32))
    wd = np.ascontiguousarray(np.asarray(L0_ffn_w_down, dtype=np.float32))
    cst = _consts()

    nc = build(T=16, NST=2, debug=False)
    in_maps = [dict(x=x[cid], wqkv=wqkv, wproj=wproj, bproj=bproj,
                    wg=wg, wu=wu, wd=wd, cst=cst) for cid in range(8)]
    res = run_bass_kernel_spmd(nc, in_maps, core_ids=list(range(8)))
    out = np.stack([r["y"] for r in res.results], axis=0)
    return out.astype(np.float32)
